# version 15
# baseline (speedup 1.0000x reference)
import contextlib
import math
import ml_dtypes
import numpy as np
import concourse.bass as bass
import concourse.mybir as mybir
from concourse.bass_utils import run_bass_kernel_spmd

F32 = mybir.dt.float32
BF16 = mybir.dt.bfloat16
I32 = mybir.dt.int32
U8 = mybir.dt.uint8
AF = mybir.ActivationFunctionType
ALU = mybir.AluOpType
AX = mybir.AxisListType

T = 8192
D = 2048
OWN = 2048
HID = 5632
NCH = 16
NHT = 44
EXPS = 1.0 / math.sqrt(128.0)
LN_EPS = 1e-5


class Sem:
    def __init__(self, name):
        self.name = name
        self.h = None
        self.cnt = 0


def build_program(dbg=False, dbgH=0, phases="all"):
    nc = bass.Bass("TRN2", target_bir_lowering=False, debug=False)
    SCR = "ExternalOutput" if dbg else "Internal"

    x_in = nc.dram_tensor("x_full", [T, D], F32, kind="ExternalInput").ap()
    x_own = nc.dram_tensor("x_own", [OWN, D], F32, kind="ExternalInput").ap()
    wq = nc.dram_tensor("wq", [D, 2048], F32, kind="ExternalInput").ap()
    wk = nc.dram_tensor("wk", [D, 512], F32, kind="ExternalInput").ap()
    wv = nc.dram_tensor("wv", [D, 512], F32, kind="ExternalInput").ap()
    wo = nc.dram_tensor("wo", [D, D], F32, kind="ExternalInput").ap()
    bq_t = nc.dram_tensor("bq_t", [128, 16], F32, kind="ExternalInput").ap()
    bk_t = nc.dram_tensor("bk_t", [128, 4], F32, kind="ExternalInput").ap()
    bv_bc = nc.dram_tensor("bv_bc", [128, 512], F32, kind="ExternalInput").ap()
    bo_bc = nc.dram_tensor("bo_bc", [128, 2048], F32, kind="ExternalInput").ap()
    wg_d = nc.dram_tensor("wg8", [128, 44, 16, 128], U8, kind="ExternalInput").ap()
    wu_d = nc.dram_tensor("wu8", [128, 44, 16, 128], U8, kind="ExternalInput").ap()
    wd_d = nc.dram_tensor("wd8", [128, 8, 11, 4, 256], U8, kind="ExternalInput").ap()
    sg_t = nc.dram_tensor("sg_t", [128, 44], F32, kind="ExternalInput").ap()
    su_t = nc.dram_tensor("su_t", [128, 44], F32, kind="ExternalInput").ap()
    sd_bc = nc.dram_tensor("sd_bc", [128, 2048], F32, kind="ExternalInput").ap()
    kcos_in = nc.dram_tensor("kcos", [128, T], F32, kind="ExternalInput").ap()
    ksin_in = nc.dram_tensor("ksin", [128, T], F32, kind="ExternalInput").ap()
    qcos_in = nc.dram_tensor("qcos", [128, OWN], F32, kind="ExternalInput").ap()
    qsin_in = nc.dram_tensor("qsin", [128, OWN], F32, kind="ExternalInput").ap()
    masks_in = nc.dram_tensor("masks", [128, 512], BF16, kind="ExternalInput").ap()
    ident_in = nc.dram_tensor("ident", [128, 128], F32, kind="ExternalInput").ap()
    sel_in = nc.dram_tensor("sel", [128, 32], F32, kind="ExternalInput").ap()

    out = nc.dram_tensor("out", [OWN, D], F32, kind="ExternalOutput").ap()

    if dbg:
        dbg_p = nc.dram_tensor("dbg_p", [16, 128, 1024], BF16, kind="ExternalOutput").ap()
        dbg_av = nc.dram_tensor("dbg_av", [8, 128, 129], F32, kind="ExternalOutput").ap()
        dbg_q = nc.dram_tensor("dbg_q", [128, 8 * OWN], BF16, kind="ExternalOutput").ap()
        dbg_f2 = nc.dram_tensor("dbg_f2", [128, 8 * OWN], BF16, kind="ExternalOutput").ap()
        dbg_cc = nc.dram_tensor("dbg_cc", [128, 16], F32, kind="ExternalOutput").ap()
        dbg_gu = nc.dram_tensor("dbg_gu", [128, 44 * 1024], BF16, kind="ExternalOutput").ap()
        dbg_wg = nc.dram_tensor("dbg_wg", [128, 2048], BF16, kind="ExternalOutput").ap()
        dbg_xr = nc.dram_tensor("dbg_xr", [1, 1024], BF16, kind="ExternalOutput").ap()
        dbg_wd = nc.dram_tensor("dbg_wd", [128, 2112], BF16, kind="ExternalOutput").ap()
        dbg_d8 = nc.dram_tensor("dbg_d8", [128, 2048], U8, kind="ExternalOutput").ap()
    kt_scr = nc.dram_tensor("kt_scr", [4, 128, T], BF16, kind=SCR).ap()
    v_scr = nc.dram_tensor("v_scr", [4, 128, 64 * 129], BF16, kind=SCR).ap()
    xo_scr = nc.dram_tensor("xo_scr", [16, 128, OWN], BF16, kind=SCR).ap()
    ff_scr = nc.dram_tensor("ff_scr", [16, 128, D], F32, kind=SCR).ap()
    ft_scr = nc.dram_tensor("ft_scr", [8, 128, OWN], BF16, kind=SCR).ap()
    xo_pv = xo_scr.rearrange("a p c -> p a c")

    def at(name, off, shape, dt):
        return nc.alloc_sbuf_tensor_at(name, shape, dt, offset=off).ap()

    BASE = 16896
    A0, B0_, C0, F0 = BASE, BASE + 94208, BASE + 126976, BASE + 159744
    FP = F0

    def fp_(name, shape, dt, nbytes):
        nonlocal FP
        ap = at(name, FP, shape, dt)
        FP += (nbytes + 31) & ~31
        return ap

    masks = fp_("masks_s", [128, 512], BF16, 1024)
    bvb = fp_("bvb", [128, 512], F32, 2048)
    bqs = fp_("bqs", [128, 16], F32, 64)
    bks = fp_("bks", [128, 4], F32, 16)
    sgs = fp_("sgs", [128, 44], F32, 176)
    sus = fp_("sus", [128, 44], F32, 176)
    eps_c = fp_("eps_c", [128, 1], F32, 4)
    cb_c = fp_("cb_c", [1, 1], F32, 4)
    mk2 = fp_("mk2", [1, 4], F32, 16)
    tn = fp_("tn", [1, 8], F32, 32)
    tnb = fp_("tnb", [1, 2], BF16, 4)
    ccol = fp_("ccol", [128, 16], F32, 64)
    ident_f = fp_("ident_f", [128, 128], F32, 512)
    ident_bf = fp_("ident_bf", [128, 128], BF16, 256)
    sel_f = fp_("sel_f", [128, 32], F32, 128)
    sel_bf = fp_("sel_bf", [128, 32], BF16, 64)
    ones_bf = fp_("ones_bf", [128, 1], BF16, 4)
    negr_bf = fp_("negr_bf", [1, 128], BF16, 256)
    m128n = fp_("m128n", [1, 128], BF16, 256)
    ones1r = fp_("ones1r", [1, 128], BF16, 256)
    bvr_bf = fp_("bvr_bf", [1, 512], BF16, 1024)
    xrow = fp_("xrow", [1, 1024], F32, 4096)
    xrow_bf = fp_("xrow_bf", [1, 1024], BF16, 2048)
    rinv = fp_("rinv", [128, 8], F32, 32)
    s1c = [fp_(f"s1c{i}", [128, 1], F32, 4) for i in range(2)]
    s2c = [fp_(f"s2c{i}", [128, 1], F32, 4) for i in range(2)]
    s3c = [fp_(f"s3c{i}", [128, 1], F32, 4) for i in range(2)]
    s4c = [fp_(f"s4c{i}", [128, 1], F32, 4) for i in range(2)]
    mcell = [fp_(f"mcell{i}", [128, 1], F32, 4) for i in range(2)]
    vcell = [fp_(f"vcell{i}", [128, 1], F32, 4) for i in range(2)]
    tcell = [fp_(f"tcell{i}", [128, 1], F32, 4) for i in range(2)]
    rcell = [fp_(f"rcell{i}", [128, 1], F32, 4) for i in range(2)]
    FP = (FP + 63) & ~63
    FL = FP
    assert FL - F0 <= 13824, FL - F0

    xnb = [at(f"xnb{i}", A0 + 4096 * i, [128, D], BF16) for i in range(2)]
    wkv = at("wkv", A0 + 16384, [128, 16 * 1024], BF16)
    wkv3 = wkv.rearrange("p (a c) -> p a c", a=16)
    xot = [at(f"xot{i}", A0 + 49152 + 4096 * i, [128, 16 * 128], BF16) for i in range(2)]
    xot3 = [x.rearrange("p (a c) -> p a c", a=16) for x in xot]
    vst = [at(f"vst{i}", A0 + 57344 + 4160 * i, [128, 4 * 516], BF16) for i in range(2)]
    sqkb = at("sqkb", A0 + 65664, [128, 2048], BF16)
    xnT = at("xnT", B0_, [128, 16 * 512], BF16)
    xnT3 = xnT.rearrange("p (a c) -> p a c", a=16)
    ktst = [at(f"ktst{i}", B0_ + 16384 + 1024 * i, [128, 512], BF16) for i in range(2)]
    xb = [at(f"xb{i}", FL + 8192 * i, [128, D], F32) for i in range(2)]
    wkv_f = at("wkv_f", FL + 16384, [128, 1024], F32)
    rp_a = at("rp_a", FL + 20480, [128, 512], F32)
    rp_b = at("rp_b", FL + 22528, [128, 512], F32)
    rp_c = at("rp_c", FL + 24576, [128, 512], F32)
    tbl = [at(f"tbl{i}", FL + 26624 + 4096 * i, [128, 1024], F32) for i in range(2)]
    sqst = at("sqst", FL + 34816, [128, 1024], F32)

    guT = at("guT", A0, [128, 44 * 1024], BF16)
    gu3 = guT.rearrange("p (a c) -> p a c", a=44)
    xnf = at("xnf", B0_, [128, 16 * 1024], BF16)
    xnf3 = xnf.rearrange("p (a c) -> p a c", a=16)
    wgbx = [at(f"wgbx{i}", C0 + 4096 * i, [128, 2048], BF16) for i in range(2)]
    wubx = [at(f"wubx{i}", C0 + 8192 + 4096 * i, [128, 2048], BF16) for i in range(2)]
    wdbx = at("wdbx", C0 + 16384, [128, 2 * 4 * 264], BF16)
    wdbx4 = wdbx.rearrange("p (s a c) -> p s a c", s=2, a=4)
    wg8 = [at(f"wg8_{i}", C0 + 20608 + 2048 * i, [128, 2048], U8) for i in range(2)]
    wu8 = [at(f"wu8_{i}", C0 + 24704 + 2048 * i, [128, 2048], U8) for i in range(2)]
    wd8s = [at(f"wd8_{i}", C0 + 28800 + 1024 * i, [128, 1024], U8) for i in range(2)]
    wd8v = [w.rearrange("p (a c) -> p a c", a=4) for w in wd8s]
    fgo = [at(f"fgo{i}", FL + 4096 + 2048 * i, [128, 512], F32) for i in range(4)]
    dst = [at(f"dst{i}", FL + 16384 + 1024 * i, [128, 256], F32) for i in range(2)]
    sdb = at("sdb", FL + 18432, [128, 2048], F32)
    dcol = at("dcol", FL + 26624, [128, 8], F32)

    kth = at("kth", A0, [128, 2 * T], BF16)
    vh = at("vh", A0 + 32768, [128, 2 * 64 * 129], BF16)
    ptb = [at(f"ptb{i}", A0 + 65792 + 2048 * i, [128, 1024], BF16) for i in range(2)]
    fstg = [at(f"fstg{i}", A0 + 69888 + 2048 * i, [128, 1024], BF16) for i in range(2)]
    attb = at("attb", A0 + 73984, [128, 8 * 128], BF16)
    sqa = at("sqa", A0 + 76032, [128, 512], BF16)
    xnq = at("xnq", A0 + 77056, [128, 16 * 512], BF16)
    xnq3 = xnq.rearrange("p (a c) -> p a c", a=16)
    qt = at("qt", B0_, [128, 8 * OWN], BF16)
    wqb8 = at("wqb8", C0, [128, 16 * 1024], BF16)
    wqb83 = wqb8.rearrange("p (a c) -> p a c", a=16)
    featT2 = at("featT2", C0, [128, 8 * OWN], BF16)
    f23 = featT2.rearrange("p (a c) -> p a c", a=8)
    ra_a = at("ra_a", FL, [128, 512], F32)
    ra_b = at("ra_b", FL + 2048, [128, 512], F32)
    ra_c = at("ra_c", FL + 4096, [128, 512], F32)
    qrope = at("qrope", FL + 6144, [128, 2 * OWN], F32)
    wqf = at("wqf", FL + 22528, [128, 2048], F32)

    featT1 = at("featT1", A0, [128, 8 * OWN], BF16)
    f13 = featT1.rearrange("p (a c) -> p a c", a=8)
    wobp = [at(f"wob{i}", A0 + 32768 + 16384 * i, [128, 16 * 512], BF16)
            for i in range(2)]
    wob3p = [w.rearrange("p (a c) -> p a c", a=16) for w in wobp]
    wof = [at("wof0", FL, [128, 512], F32),
           at("wof1", FL + 22528, [128, 512], F32)]
    xo4 = [at(f"xo4_{i}", FL + 2048 + 2048 * i, [128, 512], F32) for i in range(2)]
    yf4 = [at(f"yf4_{i}", FL + 6144 + 2048 * i, [128, 512], F32) for i in range(2)]
    ost = [at(f"ost{i}", FL + 10240 + 2048 * i, [128, 512], F32) for i in range(2)]
    bob = at("bob", FL + 14336, [128, 2048], F32)

    P = nc.alloc_psum_tensor("P", [128, 4096], F32).ap()
    BK = [P[:, 512 * i: 512 * (i + 1)] for i in range(8)]
    AVOFF = [2048, 2177, 2306, 2560, 2689, 2818, 3072, 3201]
    AVF = [P[:, o:o + 129] for o in AVOFF]

    E = {"sync": [], "gpsimd": [], "vector": [], "scalar": [], "tensor": []}
    SEMS = []

    def mks(name):
        s = Sem(name)
        SEMS.append(s)
        return s

    sPE, sV, sS = mks("sPE"), mks("sV"), mks("sS")
    qC, qWKV, qSD, qXN, qKV, qKVg, qXQ, qP4a, qP4b, sDBG = (
        mks(n) for n in
        ["qC", "qWKV", "qSD", "qXN", "qKV", "qKVg", "qXQ", "qP4a", "qP4b", "sDBG"])
    qG2 = [mks(f"qG2{i}") for i in range(2)]
    qU = [mks(f"qU{i}") for i in range(2)]
    qW2 = [mks(f"qW2{i}") for i in range(2)]
    qTBL = [mks(f"qTBL{i}") for i in range(2)]
    qX = [mks(f"qX{i}") for i in range(2)]
    qQ = [mks(f"qQ{i}") for i in range(2)]
    qWO = [mks(f"qWO{i}") for i in range(2)]
    qXY = [mks(f"qXY{i}") for i in range(2)]
    sXOT = [mks(f"sXOT{i}") for i in range(2)]
    sKT = [mks(f"sKT{i}") for i in range(2)]
    sVS = [mks(f"sVS{i}") for i in range(2)]
    sDS = [mks(f"sDS{i}") for i in range(2)]
    sFS = [mks(f"sFS{i}") for i in range(2)]
    sOUT = [mks(f"sOUT{i}") for i in range(2)]
    E2S = {"vector": sV, "scalar": sS, "tensor": sPE}

    def wait(eng, sem, val):
        if val > 0:
            E[eng].append(lambda e, s=sem, v=val: e.wait_ge(s.h, v))

    def run(eng, meth, *a, **kw):
        E[eng].append(lambda e, m=meth, aa=a, kk=kw: getattr(e, m)(*aa, **kk))

    def runi(eng, meth, *a, **kw):
        sem = E2S[eng]
        E[eng].append(lambda e, m=meth, aa=a, kk=kw, s=sem:
                      getattr(e, m)(*aa, **kk).then_inc(s.h, 1))
        sem.cnt += 1
        return sem.cnt

    def dma(eng, sem, out_, in_):
        E[eng].append(lambda e, o=out_, i=in_, s=sem:
                      e.dma_start(out=o, in_=i).then_inc(s.h, 16))
        sem.cnt += 16
        return sem.cnt

    def mm(out_, lhsT, rhs, start, stop, inc=False, skip=False):
        if inc:
            E["tensor"].append(lambda e, o=out_, l=lhsT, r=rhs, st=start, sp=stop,
                               sk=skip:
                               e.matmul(out=o, lhsT=l, rhs=r, start=st, stop=sp,
                                        skip_group_check=sk).then_inc(sPE.h, 1))
            sPE.cnt += 1
            return sPE.cnt
        E["tensor"].append(lambda e, o=out_, l=lhsT, r=rhs, st=start, sp=stop,
                           sk=skip:
                           e.matmul(out=o, lhsT=l, rhs=r, start=st, stop=sp,
                                    skip_group_check=sk))
        return sPE.cnt

    def barrier():
        snap = [(s, s.cnt) for s in SEMS]
        for eng in E:
            for s, v in snap:
                if s is not E2S.get(eng):
                    wait(eng, s, v)

    bank_free = {b: (sV, 0) for b in range(8)}

    def pe_wait_bank(b):
        s, v = bank_free[b]
        wait("tensor", s, v)

    dma("sync", qC, ident_f[:], ident_in[:, :])
    dma("sync", qC, sel_f[:], sel_in[:, :])
    dma("sync", qC, masks[:], masks_in[:, :])
    dma("sync", qC, bqs[:], bq_t[:, :])
    dma("sync", qC, bks[:], bk_t[:, :])
    dma("sync", qC, sgs[:], sg_t[:, :])
    dma("sync", qC, sus[:], su_t[:, :])
    ldc = dma("sync", qC, bvb[:], bv_bc[:, :])
    wait("vector", qC, ldc)
    runi("vector", "tensor_copy", out=ident_bf[:], in_=ident_f[:])
    runi("vector", "tensor_copy", out=sel_bf[:], in_=sel_f[:])
    run("vector", "memset", ones_bf[:], 1.0)
    run("vector", "memset", negr_bf[:], -1.0)
    run("vector", "memset", m128n[:], -128.0)
    run("vector", "memset", ones1r[:], 1.0)
    runi("vector", "tensor_copy", out=bvr_bf[:], in_=bvb[0:1, :])
    run("vector", "memset", eps_c[:], LN_EPS)
    run("vector", "memset", cb_c[:], -0.5 * math.log(128.0))
    run("vector", "memset", mk2[:], 0.0)
    mz = 0
    for i in range(2):
        mz = runi("vector", "memset", vst[i][:], 0.0)
    wait("vector", sV, mz)
    for i in range(2):
        for g in range(4):
            for t4 in range(4):
                o = g * 516 + t4 * 129 + 128
                run("vector", "memset", vst[i][:, o:o + 1], 1.0)
    wk3 = wk.rearrange("(a p) f -> a p f", p=128)
    wv3 = wv.rearrange("(a p) f -> a p f", p=128)
    for dt in range(16):
        dma("sync", qWKV, wkv_f[:, 0:512], wk3[dt])
        ld = dma("sync", qWKV, wkv_f[:, 512:1024], wv3[dt])
        wait("vector", qWKV, ld)
        vdone = runi("vector", "tensor_copy", out=wkv3[:, dt, :], in_=wkv_f[:, 0:1024])
        if dt < 15:
            wait("sync", sV, vdone)

    xb_free = [0, 0]
    xb_free_s = [0, 0]
    sq_red = [0, 0]
    xnb_free = [0, 0]
    ktst_sp = [0, 0]
    ktsq = [0, 0]
    vst_sp = [0, 0]
    xot_sp = [0, 0]
    tbl_used = [0, 0]
    row_used = 0
    kv_last_mm = 0
    kpb_prev = 0
    kdone = [0, 0, 0, 0]

    for r in range(NCH):
        rb = r % 2
        wait("sync", sV, tbl_used[rb])
        dma("sync", qTBL[rb], tbl[rb][:, 0:512], kcos_in[:, r * 512:(r + 1) * 512])
        ldtb = dma("sync", qTBL[rb], tbl[rb][:, 512:1024],
                   ksin_in[:, r * 512:(r + 1) * 512])

        def plan_stats(j):
            ib = j % 2
            wait("sync", sV, xb_free[ib])
            wait("sync", sS, xb_free_s[ib])
            ldx = dma("sync", qX[ib], xb[ib][:], x_in[j * 128:(j + 1) * 128, :])
            wait("vector", qX[ib], ldx)
            runi("vector", "reduce_sum", out=s1c[ib][:], in_=xb[ib][:], axis=AX.X)
            wait("scalar", qX[ib], ldx)
            cells = [s2c[ib], tcell[ib], s3c[ib], s4c[ib]]
            red4 = 0
            for q4 in range(4):
                half = (q4 % 2) * 512
                wait("scalar", sV, sq_red[q4 % 2])
                sq4 = runi("scalar", "activation", out=sqst[:, half:half + 512],
                           in_=xb[ib][:, q4 * 512:(q4 + 1) * 512], func=AF.Square)
                wait("vector", sS, sq4)
                red4 = runi("vector", "reduce_sum", out=cells[q4][:],
                            in_=sqst[:, half:half + 512], axis=AX.X)
                sq_red[q4 % 2] = red4
            xb_free_s[ib] = sq4
            wait("vector", sV, red4)
            run("vector", "tensor_scalar", out=mcell[ib][:], in0=s1c[ib][:],
                scalar1=1.0 / D, scalar2=None, op0=ALU.mult)
            run("vector", "tensor_tensor", out=s2c[ib][:], in0=s2c[ib][:],
                in1=tcell[ib][:], op=ALU.add)
            anc = runi("vector", "tensor_tensor", out=s3c[ib][:], in0=s3c[ib][:],
                       in1=s4c[ib][:], op=ALU.add)
            wait("vector", sV, anc)
            run("vector", "tensor_tensor", out=s2c[ib][:], in0=s2c[ib][:],
                in1=s3c[ib][:], op=ALU.add)
            anc = runi("vector", "tensor_tensor", out=tcell[ib][:], in0=mcell[ib][:],
                       in1=s1c[ib][:], op=ALU.mult)
            wait("vector", sV, anc)
            vv = runi("vector", "tensor_tensor", out=vcell[ib][:], in0=s2c[ib][:],
                      in1=tcell[ib][:], op=ALU.subtract)
            wait("scalar", sV, vv)
            rr0 = runi("scalar", "activation", out=rcell[ib][:], in_=vcell[ib][:],
                       func=AF.Ln, bias=eps_c[:], scale=1.0 / D)
            wait("scalar", sS, rr0)
            return runi("scalar", "activation", out=rcell[ib][:], in_=rcell[ib][:],
                        func=AF.Exp, scale=-0.5)

        if r == 0:
            rr_pend = plan_stats(0)
        for jj in range(4):
            j = 4 * r + jj
            ib = j % 2
            rr = rr_pend
            if j + 1 < 4 * NCH:
                rr_pend = plan_stats(j + 1)
            wait("vector", sS, rr)
            ap = runi("vector", "tensor_scalar", out=xnb[ib][:], in0=xb[ib][:],
                      scalar1=mcell[ib][:], scalar2=rcell[ib][:],
                      op0=ALU.subtract, op1=ALU.mult)
            xb_free[ib] = ap

            wait("tensor", sV, ap)
            for dtg in range(4):
                tb = (j * 4 + dtg) % 2
                pe_wait_bank(tb)
                tmm = 0
                for k in range(4):
                    dt = dtg * 4 + k
                    tmm = mm(BK[tb][:, k * 128:(k + 1) * 128],
                             xnb[ib][:, dt * 128:(dt + 1) * 128], ident_bf[:],
                             start=(k == 0), stop=(k == 3), inc=(k == 3))
                wait("scalar", sPE, tmm)
                if r > 0 and jj == 0 and dtg == 0:
                    wait("scalar", sPE, kv_last_mm)
                ev = runi("scalar", "copy",
                          out=xnT3[:, dtg * 4:(dtg + 1) * 4, jj * 128:(jj + 1) * 128],
                          in_=BK[tb].rearrange("p (a c) -> p a c", a=4))
                bank_free[tb] = (sS, ev)
            pe_wait_bank(6)
            smm = 0
            for dt in range(16):
                smm = mm(BK[6][:, dt * 32:(dt + 1) * 32],
                         xnb[ib][:, dt * 128:(dt + 1) * 128], sel_bf[:],
                         start=(dt == 0), stop=(dt == 15), inc=(dt == 15))
            xnb_free[ib] = smm
            wait("scalar", sPE, smm)
            if jj == 0:
                wait("scalar", sXOT[rb], xot_sp[rb])
            sev = runi("scalar", "copy",
                       out=xot3[rb][:, :, jj * 32:(jj + 1) * 32],
                       in_=BK[6].rearrange("p (a c) -> p a c", a=16))
            bank_free[6] = (sS, sev)
        xnT_evac = sS.cnt

        wait("gpsimd", sS, xnT_evac)
        xot_sp[rb] = dma("gpsimd", sXOT[rb], xo_pv[:, :, r * 128:(r + 1) * 128],
                         xot3[rb][:, :, :])

        wait("tensor", sS, xnT_evac)
        for g in range(4):
            kb = 2 + (g % 2)
            pe_wait_bank(kb)
            kmm = 0
            for dt in range(16):
                kmm = mm(BK[kb], wkv3[:, dt, g * 128:(g + 1) * 128], xnT3[:, dt, :],
                         start=(dt == 0), stop=(dt == 15), inc=(dt == 15))
            wait("scalar", sPE, kmm)
            wait("scalar", sV, kpb_prev)
            pa = runi("scalar", "activation", out=rp_a[:], in_=BK[kb],
                      func=AF.Identity, bias=bks[:, g:g + 1])
            bank_free[kb] = (sS, pa)
            wait("scalar", sS, pa)
            run("scalar", "copy", out=rp_b[0:64, :], in_=rp_a[64:128, :])
            pb = runi("scalar", "copy", out=rp_b[64:128, :], in_=rp_a[0:64, :])
            wait("vector", qTBL[rb], ldtb)
            wait("vector", sS, pb)
            wait("vector", sKT[g % 2], ktst_sp[g % 2])
            wait("vector", sS, ktsq[g % 2])
            run("vector", "tensor_tensor", out=rp_c[:], in0=rp_a[:],
                in1=tbl[rb][:, 0:512], op=ALU.mult)
            anc = runi("vector", "tensor_tensor", out=rp_b[:], in0=rp_b[:],
                       in1=tbl[rb][:, 512:1024], op=ALU.mult)
            kpb_prev = anc
            wait("vector", sV, anc)
            kdone[g] = runi("vector", "tensor_tensor", out=ktst[g % 2][:],
                            in0=rp_c[:], in1=rp_b[:], op=ALU.add)
            wait("gpsimd", sV, kdone[g])
            ktst_sp[g % 2] = dma("gpsimd", sKT[g % 2],
                                 kt_scr[g][:, r * 512:(r + 1) * 512],
                                 ktst[g % 2][:])
            wait("scalar", sV, kdone[g])
            ktsq[g % 2] = runi("scalar", "activation",
                               out=sqkb[:, g * 512:(g + 1) * 512],
                               in_=ktst[g % 2][:], func=AF.Square)
        tbl_used[rb] = sV.cnt
        sq_all = sS.cnt

        for t4 in range(4):
            vb = 4 + (t4 % 2)
            pe_wait_bank(vb)
            for dt in range(16):
                mm(BK[vb], xnT3[:, dt, t4 * 128:(t4 + 1) * 128],
                   wkv3[:, dt, 512:1024], start=(dt == 0), stop=False)
            vmm = mm(BK[vb], ones1r[:], bvr_bf[:], start=False, stop=True, inc=True)
            wait("scalar", sPE, vmm)
            if t4 == 0:
                wait("scalar", sVS[rb], vst_sp[rb])
            vev = 0
            for g in range(4):
                vev = runi("scalar", "copy",
                           out=vst[rb][:, g * 516 + t4 * 129: g * 516 + t4 * 129 + 128],
                           in_=BK[vb][:, g * 128:(g + 1) * 128])
            bank_free[vb] = (sS, vev)
        kv_last_mm = sPE.cnt
        wait("gpsimd", sS, sS.cnt)
        for g in range(4):
            vst_sp[rb] = dma("gpsimd", sVS[rb], v_scr[g][:, r * 516:(r + 1) * 516],
                             vst[rb][:, g * 516:(g + 1) * 516])

        for half in range(2):
            g0, g1 = 2 * half, 2 * half + 1
            wait("tensor", sS, sq_all)
            wait("tensor", sV, row_used)
            if half == 0:
                s_, v_ = bank_free[6]
                wait("tensor", s_, v_)
            mm(P[0:1, 3072:3584], ones_bf[:], sqkb[:, g0 * 512:(g0 + 1) * 512],
               start=True, stop=True)
            rs = mm(P[0:1, 3584:4096], ones_bf[:], sqkb[:, g1 * 512:(g1 + 1) * 512],
                    start=True, stop=True, inc=True)
            wait("vector", sPE, rs)
            run("vector", "reduce_max", out=tn[0:1, 0:1],
                in_=P[0:1, 3072:3584], axis=AX.X)
            rmx = runi("vector", "reduce_max", out=tn[0:1, 1:2],
                       in_=P[0:1, 3584:4096], axis=AX.X)
            wait("vector", sV, rmx)
            run("vector", "tensor_tensor", out=mk2[0:1, g0:g0 + 1],
                in0=mk2[0:1, g0:g0 + 1], in1=tn[0:1, 0:1], op=ALU.max)
            row_used = runi("vector", "tensor_tensor", out=mk2[0:1, g1:g1 + 1],
                            in0=mk2[0:1, g1:g1 + 1], in1=tn[0:1, 1:2], op=ALU.max)
            bank_free[6] = (sV, row_used)

    barrier()
    ldsd = dma("sync", qSD, sdb[:], sd_bc[:, :])
    wmset = runi("vector", "memset", wdbx[:], 128.0)
    dst_sp = [0, 0]
    g8_free = [0, 0]
    u8_free = [0, 0]
    d8_free = [0, 0]
    gmm_ht = [0, 0]
    dmm_sl = [0, 0]
    gq_par = [0, 0]

    gate_mm_last = 0
    for F in range(2):
        if F > 0:
            wait("sync", sPE, gate_mm_last)
        ldxn = dma("sync", qXN, xnf3[:, :, :], xo_pv[:, :, F * 1024:(F + 1) * 1024])
        wait("tensor", qXN, ldxn)
        xcp = 0
        for tc in range(2):
            wait("tensor", sV, xcp)
            if tc == 0:
                s_, v_ = bank_free[7]
                wait("tensor", s_, v_)
            xr = 0
            for dt in range(16):
                xr = mm(P[0:1, 3584:4096], ones_bf[:],
                        xnf3[:, dt, tc * 512:(tc + 1) * 512],
                        start=(dt == 0), stop=(dt == 15), inc=(dt == 15))
            wait("vector", sPE, xr)
            xcp = runi("vector", "tensor_copy", out=xrow[0:1, tc * 512:(tc + 1) * 512],
                       in_=P[0:1, 3584:4096])
        wait("vector", sV, xcp)
        xrc = runi("vector", "tensor_copy", out=xrow_bf[:], in_=xrow[:])
        bank_free[7] = (sV, xcp)

        for ht in range(NHT):
            pp = ht % 2
            wait("sync", sV, g8_free[pp])
            ldg = dma("sync", qG2[pp], wg8[pp][:], wg_d[:, ht])
            wait("sync", sV, u8_free[pp])
            ldu = dma("sync", qU[pp], wu8[pp][:], wu_d[:, ht])
            wait("vector", qG2[pp], ldg)
            wait("vector", sPE, gmm_ht[pp])
            g8_free[pp] = runi("vector", "tensor_copy", out=wgbx[pp][:],
                               in_=wg8[pp][:])
            wait("vector", qU[pp], ldu)
            ucst = runi("vector", "tensor_copy", out=wubx[pp][:], in_=wu8[pp][:])
            u8_free[pp] = ucst
            wait("tensor", sV, ucst)
            if ht == 0:
                wait("tensor", sV, xrc)
            for tc in range(2):
                gb, ub = tc, 2 + tc
                pe_wait_bank(gb)
                for dt in range(16):
                    mm(BK[gb], wgbx[pp][:, dt * 128:(dt + 1) * 128],
                       xnf3[:, dt, tc * 512:(tc + 1) * 512],
                       start=(dt == 0), stop=False)
                gmm = mm(BK[gb], m128n[:], xrow_bf[0:1, tc * 512:(tc + 1) * 512],
                         start=False, stop=True, inc=True)
                pe_wait_bank(ub)
                for dt in range(16):
                    mm(BK[ub], wubx[pp][:, dt * 128:(dt + 1) * 128],
                       xnf3[:, dt, tc * 512:(tc + 1) * 512],
                       start=(dt == 0), stop=False)
                umm = mm(BK[ub], m128n[:], xrow_bf[0:1, tc * 512:(tc + 1) * 512],
                         start=False, stop=True, inc=True)
                gmm_ht[pp] = umm
                gate_mm_last = umm
                fg, fu = fgo[2 * (tc % 2)], fgo[2 * (tc % 2) + 1]
                wait("scalar", sPE, gmm)
                wait("scalar", sV, gq_par[tc % 2])
                sg1 = runi("scalar", "activation", out=fg[:], in_=BK[gb],
                           func=AF.Silu, scale=sgs[:, ht:ht + 1])
                bank_free[gb] = (sS, sg1)
                wait("scalar", sPE, umm)
                sact = runi("scalar", "mul", out=fu[:], in_=BK[ub],
                            mul=sus[:, ht:ht + 1])
                bank_free[ub] = (sS, sact)
                wait("vector", sS, sact)
                gq_par[tc % 2] = runi("vector", "tensor_tensor",
                                      out=gu3[:, ht, tc * 512:(tc + 1) * 512],
                                      in0=fg[:], in1=fu[:], op=ALU.mult)
        gu_done = sV.cnt
        if dbg and F == 1:
            wait("gpsimd", sV, sV.cnt)
            dma("gpsimd", sDBG, dbg_gu[:, :], guT[:, :])
            dma("gpsimd", sDBG, dbg_wg[:, :], wgbx[1][:])
            dma("gpsimd", sDBG, dbg_xr[:, :], xrow_bf[:, :])

        DT = [P[:, 512 * b: 512 * b + 257] for b in range(8)]
        wait("tensor", sV, gu_done)
        if F == 0:
            wait("vector", qSD, ldsd)
        stopc = [0] * 8

        def plan_evac(dcp):
            for tt in range(8):
                wait("scalar", sPE, stopc[tt])
                cpy = runi("scalar", "copy", out=dcol[:, tt:tt + 1],
                           in_=DT[tt][:, 256:257])
                wait("vector", sDS[tt % 2], dst_sp[tt % 2])
                wait("vector", sS, cpy)
                dv = runi("vector", "scalar_tensor_tensor", out=dst[tt % 2][:],
                          in0=DT[tt][:, 0:256], scalar=dcol[:, tt:tt + 1],
                          in1=sdb[:, dcp * 256:(dcp + 1) * 256],
                          op0=ALU.subtract, op1=ALU.mult)
                bank_free[tt] = (sV, dv)
                wait("gpsimd", sV, dv)
                dst_sp[tt % 2] = dma("gpsimd", sDS[tt % 2],
                                     ff_scr[F * 8 + tt][:, dcp * 256:(dcp + 1) * 256],
                                     dst[tt % 2][:])

        for dc in range(8):
            for htg in range(11):
                sl = (dc * 11 + htg) % 2
                wait("sync", sV, d8_free[sl])
                ldw = dma("sync", qW2[sl], wd8s[sl][:], wd_d[:, dc, htg])
                wait("vector", qW2[sl], ldw)
                wait("vector", sPE, dmm_sl[sl])
                d8_free[sl] = runi("vector", "tensor_copy",
                                   out=wdbx4[:, sl, :, 0:256], in_=wd8v[sl][:, :, :])
                if htg == 0 and dc > 0:
                    plan_evac(dc - 1)
                wait("tensor", sV, d8_free[sl])
                dmm = 0
                for tt in range(8):
                    if htg == 0:
                        s_, v_ = bank_free[tt]
                        wait("tensor", s_, v_)
                    for a in range(4):
                        dmm = mm(DT[tt], gu3[:, htg * 4 + a, tt * 128:(tt + 1) * 128],
                                 wdbx4[:, sl, a, 0:257],
                                 start=(htg == 0 and a == 0),
                                 stop=(htg == 10 and a == 3),
                                 inc=(a == 3 and (tt == 7 or htg == 10)))
                    if htg == 10:
                        stopc[tt] = dmm
                dmm_sl[sl] = dmm
        plan_evac(7)
        if dbg and F == 1:
            wait("gpsimd", sV, sV.cnt)
            dma("gpsimd", sDBG, dbg_wd[:, :], wdbx[:, :])
            dma("gpsimd", sDBG, dbg_d8[:, 0:1024], wd8s[0][:, :])
            dma("gpsimd", sDBG, dbg_d8[:, 1024:2048], wd8s[1][:, :])

    if phases == "all":
        qr = [0, 0]
        for H in range(2):
            barrier()
            if H == 0:
                dma("scalar", qKV, qrope[:, 0:OWN], qcos_in[:, :])
                dma("scalar", qKV, qrope[:, OWN:2 * OWN], qsin_in[:, :])
            rope_cnt = qKV.cnt
            ldxq0 = dma("sync", qXQ, xnq3[:, :, :], xo_pv[:, :, 0:512])
            for gl in range(2):
                dma("scalar", qKVg, kth[:, gl * T:(gl + 1) * T],
                    kt_scr[2 * H + gl][:, :])
                dma("scalar", qKVg, vh[:, gl * 8256:(gl + 1) * 8256],
                    v_scr[2 * H + gl][:, :])
            kvg_cnt = qKVg.cnt
            qr_dt = [0] * 16
            for dt in range(16):
                wait("sync", sV, qr[dt % 2])
                ldq = dma("sync", qQ[dt % 2],
                          wqf[:, (dt % 2) * 1024:(dt % 2) * 1024 + 1024],
                          wq[dt * 128:(dt + 1) * 128, H * 1024:(H + 1) * 1024])
                wait("vector", qQ[dt % 2], ldq)
                qr[dt % 2] = runi("vector", "tensor_copy", out=wqb83[:, dt, :],
                                  in_=wqf[:, (dt % 2) * 1024:(dt % 2) * 1024 + 1024])
                qr_dt[dt] = qr[dt % 2]

            qdone_prev = 0
            qpb_prev = 0
            qmm_last = 0
            for qch in range(4):
                if qch == 0:
                    ldxq = ldxq0
                else:
                    wait("sync", sPE, qmm_last)
                    ldxq = dma("sync", qXQ, xnq3[:, :, :],
                               xo_pv[:, :, qch * 512:(qch + 1) * 512])
                wait("tensor", qXQ, ldxq)
                for hl in range(8):
                    h = 8 * H + hl
                    qb = hl % 2
                    pe_wait_bank(qb)
                    qmm = 0
                    for dt in range(16):
                        if qch == 0 and hl == 0:
                            wait("tensor", sV, qr_dt[dt])
                        qmm = mm(BK[qb], wqb83[:, dt, hl * 128:(hl + 1) * 128],
                                 xnq3[:, dt, :], start=(dt == 0), stop=(dt == 15),
                                 inc=(dt == 15))
                    qmm_last = qmm
                    wait("vector", sPE, qmm)
                    wait("vector", sS, qpb_prev)
                    pbv = runi("vector", "tensor_scalar", out=ra_a[:], in0=BK[qb],
                               scalar1=bqs[:, h:h + 1], scalar2=None, op0=ALU.add)
                    bank_free[qb] = (sV, pbv)
                    wait("scalar", sV, pbv)
                    run("scalar", "copy", out=ra_b[0:64, :], in_=ra_a[64:128, :])
                    pb = runi("scalar", "copy", out=ra_b[64:128, :], in_=ra_a[0:64, :])
                    qpb_prev = pb
                    wait("vector", sS, pb)
                    if H == 0 and qch == 0 and hl == 0:
                        wait("vector", qKV, rope_cnt)
                    run("vector", "tensor_tensor", out=ra_c[:], in0=ra_a[:],
                        in1=qrope[:, qch * 512:(qch + 1) * 512], op=ALU.mult)
                    anc = runi("vector", "tensor_tensor", out=ra_b[:], in0=ra_b[:],
                               in1=qrope[:, OWN + qch * 512: OWN + (qch + 1) * 512],
                               op=ALU.mult)
                    wait("vector", sV, anc)
                    qs = qt[:, hl * OWN + qch * 512: hl * OWN + (qch + 1) * 512]
                    qdone_prev = runi("vector", "tensor_tensor", out=qs, in0=ra_c[:],
                                      in1=ra_b[:], op=ALU.add)
                    wait("vector", sV, qdone_prev)
                    sqv = runi("vector", "tensor_tensor", out=sqa[:], in0=qs, in1=qs,
                               op=ALU.mult)
                    wait("tensor", sV, sqv)
                    wait("tensor", sV, row_used)
                    rmm = mm(P[0:1, 3584:4096], ones_bf[:], sqa[:],
                             start=True, stop=True, inc=True)
                    wait("vector", sPE, rmm)
                    mq = runi("vector", "reduce_max", out=tn[0:1, hl % 2: hl % 2 + 1],
                              in_=P[0:1, 3584:4096], axis=AX.X)
                    row_used = mq
                    if hl % 2 == 1:
                        wait("vector", sV, mq)
                        anc = runi("vector", "tensor_tensor", out=tn[0:1, 2:3],
                                   in0=tn[0:1, 0:1], in1=tn[0:1, 1:2], op=ALU.max)
                        wait("vector", sV, anc)
                        g = hl // 4
                        mx = runi("vector", "tensor_tensor", out=tn[0:1, 2:3],
                                  in0=tn[0:1, 2:3], in1=mk2[0:1, 2 * H + g: 2 * H + g + 1],
                                  op=ALU.mult)
                        wait("scalar", sV, mx)
                        lg = runi("scalar", "activation", out=tn[0:1, 3:4],
                                  in_=tn[0:1, 2:3], func=AF.Ln)
                        wait("scalar", sS, lg)
                        cc = runi("scalar", "activation", out=tn[0:1, 4:5],
                                  in_=tn[0:1, 3:4], func=AF.Exp, scale=0.5, bias=cb_c[:])
                        wait("vector", sS, cc)
                        cbf = runi("vector", "tensor_copy", out=tnb[0:1, 0:1],
                                   in_=tn[0:1, 4:5])
                        wait("tensor", sV, cbf)
                        cmm = mm(P[:, 3584:3585], negr_bf[:], tnb[0:1, 0:1],
                                 start=True, stop=True, inc=True)
                        wait("vector", sPE, cmm)
                        ci = (hl // 2) * 4 + qch
                        row_used = runi("vector", "tensor_copy",
                                        out=ccol[:, ci:ci + 1], in_=P[:, 3584:3585])
            ccol_done = sV.cnt
            wait("scalar", sV, ccol_done)
            wait("tensor", sV, ccol_done)
            wait("tensor", qKVg, kvg_cnt)
            if dbg and H == dbgH:
                wait("gpsimd", sV, ccol_done)
                dma("gpsimd", sDBG, dbg_q[:, :], qt[:, :])

            sbank_free = [0, 0]
            pt_free = [0, 0]
            dbg_dma = [0, 0]
            av_free = 0
            fstg_sp = [0, 0]
            b7_free = ccol_done
            for qch in range(4):
                kts = 16 * qch + 16
                for pr in range(4):
                    hl0, hl1 = 2 * pr, 2 * pr + 1
                    gl = pr // 2
                    def plan_S(kt):
                        e = kt - 16 * qch
                        col0 = 128 * (0 if e < 0 else e // 4)
                        soff = 1024 * (kt % 2)
                        wait("tensor", sS, sbank_free[kt % 2])
                        last = 0
                        for hh in range(2):
                            hl = (hl0, hl1)[hh]
                            last = mm(P[:, soff + hh * 512 + col0: soff + (hh + 1) * 512],
                                      kth[:, gl * T + kt * 128: gl * T + (kt + 1) * 128],
                                      qt[:, hl * OWN + qch * 512 + col0: hl * OWN + (qch + 1) * 512],
                                      start=True, stop=True, inc=(hh == 1))
                        return last

                    avc = 0
                    lastS = plan_S(0)
                    for kt in range(kts):
                        e = kt - 16 * qch
                        m = 0 if e < 0 else e // 4
                        col0 = 128 * m
                        sb_i = kt % 2
                        soff = 1024 * sb_i
                        last = lastS
                        if kt + 1 < kts:
                            lastS = plan_S(kt + 1)
                        mk = 0
                        if 0 <= e < 16:
                            qi, dd = e // 4, e % 4
                            wait("vector", sPE, last)
                            for hh in range(2):
                                o = soff + hh * 512 + qi * 128
                                mk = runi("vector", "tensor_tensor", out=P[:, o:o + 128],
                                          in0=P[:, o:o + 128],
                                          in1=masks[:, dd * 128:(dd + 1) * 128], op=ALU.add)
                        wait("scalar", sPE, last)
                        if mk:
                            wait("scalar", sV, mk)
                        if kt >= 2:
                            wait("scalar", sPE, pt_free[sb_i])
                            if dbg:
                                wait("scalar", sDBG, dbg_dma[sb_i])
                        ci = pr * 4 + qch
                        if m == 0:
                            ex = runi("scalar", "activation", out=ptb[sb_i][:, 0:1024],
                                      in_=P[:, soff:soff + 1024], func=AF.Exp,
                                      scale=EXPS, bias=ccol[:, ci:ci + 1])
                        else:
                            for hh in range(2):
                                ex = runi("scalar", "activation",
                                          out=ptb[sb_i][:, hh * 512 + col0:(hh + 1) * 512],
                                          in_=P[:, soff + hh * 512 + col0: soff + (hh + 1) * 512],
                                          func=AF.Exp, scale=EXPS, bias=ccol[:, ci:ci + 1])
                        sbank_free[sb_i] = ex
                        if dbg and H == dbgH and qch == 0 and pr == 0:
                            wait("gpsimd", sS, ex)
                            dbg_dma[sb_i] = dma("gpsimd", sDBG, dbg_p[kt][:, :],
                                                ptb[sb_i][:, :])
                        if kt == 0:
                            wait("tensor", sV, av_free)
                        wait("tensor", sS, ex)
                        for hh in range(2):
                            for qi in range(m, 4):
                                sl = hh * 4 + qi
                                avc = mm(AVF[sl],
                                         ptb[sb_i][:, hh * 512 + qi * 128: hh * 512 + (qi + 1) * 128],
                                         vh[:, gl * 8256 + kt * 129: gl * 8256 + (kt + 1) * 129],
                                         start=(kt == 0 and sl in (0, 3, 6)),
                                         stop=(kt == 16 * qch + 4 * qi + 3),
                                         inc=(hh == 1 and qi == 3), skip=True)
                        pt_free[sb_i] = avc
                    wait("vector", sPE, avc)
                    if dbg and H == dbgH and qch == 0 and pr == 0:
                        for sl in range(8):
                            dcp = runi("vector", "tensor_copy", out=ra_c[:, 0:129],
                                       in_=AVF[sl])
                            wait("gpsimd", sV, dcp)
                            dga = dma("gpsimd", sDBG, dbg_av[sl][:, :], ra_c[:, 0:129])
                            wait("vector", sDBG, dga)
                    rc = 0
                    for sl in range(8):
                        rc = runi("vector", "reciprocal", out=rinv[:, sl:sl + 1],
                                  in_=AVF[sl][:, 128:129])
                    wait("vector", sV, rc)
                    fz = 0
                    for sl in range(8):
                        fz = runi("vector", "tensor_scalar",
                                  out=attb[:, sl * 128:(sl + 1) * 128],
                                  in0=AVF[sl][:, 0:128], scalar1=rinv[:, sl:sl + 1],
                                  scalar2=None, op0=ALU.mult)
                    av_free = fz
                    for hh in range(2):
                        wait("tensor", sV, b7_free)
                        wait("tensor", sV, fz)
                        ft = 0
                        for k in range(4):
                            sl = hh * 4 + k
                            ft = mm(P[:, 3584 + k * 128: 3584 + (k + 1) * 128],
                                    attb[:, sl * 128:(sl + 1) * 128], ident_bf[:],
                                    start=(k == 0), stop=(k == 3), inc=(k == 3))
                        wait("vector", sPE, ft)
                        hl = (hl0, hl1)[hh]
                        if H == 0:
                            if hh == 0:
                                wait("vector", sFS[pr % 2], fstg_sp[pr % 2])
                            fe = runi("vector", "tensor_copy",
                                      out=fstg[pr % 2][:, hh * 512:(hh + 1) * 512],
                                      in_=P[:, 3584:4096])
                        else:
                            fe = runi("vector", "tensor_copy",
                                      out=f23[:, hl, qch * 512:(qch + 1) * 512],
                                      in_=P[:, 3584:4096])
                        b7_free = fe
                    if H == 0:
                        wait("gpsimd", sV, sV.cnt)
                        for hh in range(2):
                            hl = (hl0, hl1)[hh]
                            fstg_sp[pr % 2] = dma("gpsimd", sFS[pr % 2],
                                                  ft_scr[hl][:, qch * 512:(qch + 1) * 512],
                                                  fstg[pr % 2][:, hh * 512:(hh + 1) * 512])
            row_used = sV.cnt

        if dbg:
            wait("gpsimd", sV, sV.cnt)
            dma("gpsimd", sDBG, dbg_f2[:, :], featT2[:, :])
            dma("gpsimd", sDBG, dbg_cc[:, :], ccol[:, :])

        barrier()
        dma("sync", qP4a, bob[:], bo_bc[:, :])
        for ft in range(8):
            if ft % 2 == 0:
                dma("sync", qP4a, f13[:, ft, :], ft_scr[ft][:, :])
            else:
                dma("scalar", qP4b, f13[:, ft, :], ft_scr[ft][:, :])
        wob_c = [0, 0]
        wob_mm = [0, 0]
        st_last = [0, 0]
        ring_free = [0, 0]
        for dc in range(4):
            wb = dc % 2
            for ft in range(16):
                wait("sync", sV, wob_c[ft % 2])
                ldw = dma("sync", qWO[ft % 2], wof[ft % 2][:, :],
                          wo[ft * 128:(ft + 1) * 128, dc * 512:(dc + 1) * 512])
                wait("vector", qWO[ft % 2], ldw)
                if ft == 0:
                    wait("vector", sPE, wob_mm[wb])
                wob_c[ft % 2] = runi("vector", "tensor_copy",
                                     out=wob3p[wb][:, ft, :], in_=wof[ft % 2][:])
            wcast = sV.cnt
            wait("tensor", sV, wcast)
            if dc == 0:
                wait("tensor", qP4a, qP4a.cnt)
                wait("tensor", qP4b, qP4b.cnt)
            for tt in range(16):
                wait("scalar", sV, ring_free[tt % 2])
                dma("scalar", qXY[tt % 2], xo4[tt % 2][:],
                    x_own[tt * 128:(tt + 1) * 128, dc * 512:(dc + 1) * 512])
                ldy = dma("scalar", qXY[tt % 2], yf4[tt % 2][:],
                          ff_scr[tt][:, dc * 512:(dc + 1) * 512])
                ob = tt % 2
                pe_wait_bank(ob)
                omm = 0
                for ft in range(16):
                    f3 = f13 if ft < 8 else f23
                    omm = mm(BK[ob], f3[:, ft % 8, tt * 128:(tt + 1) * 128],
                             wob3p[wb][:, ft, :], start=(ft == 0), stop=(ft == 15),
                             inc=(ft == 15))
                wob_mm[wb] = omm
                wait("vector", sPE, omm)
                wait("vector", qXY[tt % 2], ldy)
                wait("vector", sOUT[tt % 2], st_last[tt % 2])
                run("vector", "tensor_tensor", out=ost[tt % 2][:], in0=BK[ob],
                    in1=bob[:, dc * 512:(dc + 1) * 512], op=ALU.add)
                anc = runi("vector", "tensor_tensor", out=xo4[tt % 2][:],
                           in0=xo4[tt % 2][:], in1=yf4[tt % 2][:], op=ALU.add)
                wait("vector", sV, anc)
                oe = runi("vector", "tensor_tensor", out=ost[tt % 2][:],
                          in0=ost[tt % 2][:], in1=xo4[tt % 2][:], op=ALU.add)
                bank_free[ob] = (sV, oe)
                ring_free[tt % 2] = oe
                wait("gpsimd", sV, oe)
                st_last[tt % 2] = dma("gpsimd", sOUT[tt % 2],
                                      out[tt * 128:(tt + 1) * 128,
                                          dc * 512:(dc + 1) * 512],
                                      ost[tt % 2][:])
        for e_ in ("gpsimd", "sync"):
            wait(e_, sOUT[0], sOUT[0].cnt)
            wait(e_, sOUT[1], sOUT[1].cnt)
            if dbg:
                wait(e_, sDBG, sDBG.cnt)

    with contextlib.ExitStack() as stack:
        for s_ in SEMS:
            s_.h = stack.enter_context(nc.semaphore(s_.name))
        blk = stack.enter_context(nc.Block())

        @blk.sync
        def _(e):
            for f_ in E["sync"]:
                f_(e)

        @blk.gpsimd
        def _(e):
            for f_ in E["gpsimd"]:
                f_(e)

        @blk.vector
        def _(e):
            for f_ in E["vector"]:
                f_(e)

        @blk.scalar
        def _(e):
            for f_ in E["scalar"]:
                f_(e)

        @blk.tensor
        def _(e):
            for f_ in E["tensor"]:
                f_(e)

    return nc



_ROPE_BASE = 1000000.0


def _rope_tables(positions):
    inv = 1.0 / _ROPE_BASE ** (np.arange(0, 64, dtype=np.float64) * 2.0 / 128.0)
    ang = inv[:, None] * np.asarray(positions, dtype=np.float64)[None, :]
    cos = np.cos(ang)
    sin = np.sin(ang)
    kcos = np.ascontiguousarray(np.concatenate([cos, cos], axis=0).astype(np.float32))
    ksin = np.ascontiguousarray(np.concatenate([-sin, sin], axis=0).astype(np.float32))
    return kcos, ksin


def host_inmaps(L0_attn_x, L0_attn_Wq, L0_attn_bq, L0_attn_Wk, L0_attn_bk,
                L0_attn_Wv, L0_attn_bv, L0_attn_Wo, L0_attn_bo,
                L0_ffn_wq_gate, L0_ffn_s_gate, L0_ffn_wq_up, L0_ffn_s_up,
                L0_ffn_wq_down, L0_ffn_s_down):
    x = np.asarray(L0_attn_x, dtype=np.float32)
    wq = np.ascontiguousarray(np.asarray(L0_attn_Wq, dtype=np.float32))
    wk = np.ascontiguousarray(np.asarray(L0_attn_Wk, dtype=np.float32))
    wv = np.ascontiguousarray(np.asarray(L0_attn_Wv, dtype=np.float32))
    wo = np.ascontiguousarray(np.asarray(L0_attn_Wo, dtype=np.float32))
    bq = np.asarray(L0_attn_bq, dtype=np.float32)
    bk = np.asarray(L0_attn_bk, dtype=np.float32)
    bv = np.asarray(L0_attn_bv, dtype=np.float32)
    bo = np.asarray(L0_attn_bo, dtype=np.float32)
    wg = np.asarray(L0_ffn_wq_gate, dtype=np.int32)
    wu = np.asarray(L0_ffn_wq_up, dtype=np.int32)
    wd = np.asarray(L0_ffn_wq_down, dtype=np.int32)
    sg = np.asarray(L0_ffn_s_gate, dtype=np.float32)
    su = np.asarray(L0_ffn_s_up, dtype=np.float32)
    sd = np.asarray(L0_ffn_s_down, dtype=np.float32)

    wgH = np.ascontiguousarray(
        wg.astype(np.uint8).reshape(44, 128, 16, 128).transpose(3, 0, 2, 1))
    wuH = np.ascontiguousarray(
        wu.astype(np.uint8).reshape(44, 128, 16, 128).transpose(3, 0, 2, 1))
    wdH = np.ascontiguousarray(
        wd.astype(np.uint8).reshape(8, 256, 11, 4, 128).transpose(4, 0, 2, 3, 1))
    bq_t = np.ascontiguousarray(bq.reshape(16, 128).T)
    bk_t = np.ascontiguousarray(bk.reshape(4, 128).T)
    bv_bc = np.ascontiguousarray(np.broadcast_to(bv, (128, 512)))
    bo_bc = np.ascontiguousarray(np.broadcast_to(bo, (128, 2048)))
    sg_t = np.ascontiguousarray(sg.reshape(44, 128).T)
    su_t = np.ascontiguousarray(su.reshape(44, 128).T)
    sd_bc = np.ascontiguousarray(np.broadcast_to(sd, (128, 2048)))
    kcos, ksin = _rope_tables(np.arange(T))
    ident = np.eye(128, dtype=np.float32)

    in_maps = []
    for cid in range(8):
        b, c = cid // 4, cid % 4
        qcos, qsin = _rope_tables(4 * np.arange(OWN) + c)
        kp = np.arange(128)[:, None]
        jq = np.arange(128)[None, :]
        mask = np.zeros((128, 512), dtype=np.float32)
        for dblk in range(4):
            mask[:, dblk * 128:(dblk + 1) * 128] = np.where(
                128 * dblk + kp <= 4 * jq + c, 0.0, -1e9).astype(np.float32)
        sel = np.zeros((128, 32), dtype=np.float32)
        sel[4 * np.arange(32) + c, np.arange(32)] = 1.0
        in_maps.append(dict(
            x_full=np.ascontiguousarray(x[b]),
            x_own=np.ascontiguousarray(x[b, c::4]),
            wq=wq, wk=wk, wv=wv, wo=wo, bq_t=bq_t, bk_t=bk_t, bv_bc=bv_bc,
            bo_bc=bo_bc, wg8=wgH, wu8=wuH, wd8=wdH, sg_t=sg_t, su_t=su_t,
            sd_bc=sd_bc, kcos=kcos, ksin=ksin, qcos=qcos, qsin=qsin,
            masks=mask.astype(ml_dtypes.bfloat16), ident=ident, sel=sel,
        ))

    return in_maps


def kernel(**inputs):
    in_maps = host_inmaps(**inputs)
    nc = build_program()
    res = run_bass_kernel_spmd(nc, in_maps, core_ids=list(range(8)))
    full = np.empty((2, T, D), dtype=np.float32)
    for cid in range(8):
        b, c = cid // 4, cid % 4
        full[b, c::4] = res.results[cid]["out"]
    return full


if __name__ == "__main__":
    build_program()
    print("built ok")


# version 16
# speedup vs baseline: 1.0175x; 1.0175x over previous
import contextlib
import math
import ml_dtypes
import numpy as np
import concourse.bass as bass
import concourse.mybir as mybir
from concourse.bass_utils import run_bass_kernel_spmd

F32 = mybir.dt.float32
BF16 = mybir.dt.bfloat16
I32 = mybir.dt.int32
U8 = mybir.dt.uint8
AF = mybir.ActivationFunctionType
ALU = mybir.AluOpType
AX = mybir.AxisListType

T = 8192
D = 2048
OWN = 2048
HID = 5632
NCH = 16
NHT = 44
EXPS = 1.0 / math.sqrt(128.0)
LN_EPS = 1e-5


class Sem:
    def __init__(self, name):
        self.name = name
        self.h = None
        self.cnt = 0


def build_program(dbg=False, dbgH=0, phases="all"):
    nc = bass.Bass("TRN2", target_bir_lowering=False, debug=False)
    SCR = "ExternalOutput" if dbg else "Internal"

    x_in = nc.dram_tensor("x_full", [T, D], F32, kind="ExternalInput").ap()
    x_own = nc.dram_tensor("x_own", [OWN, D], F32, kind="ExternalInput").ap()
    wq = nc.dram_tensor("wq", [D, 2048], F32, kind="ExternalInput").ap()
    wk = nc.dram_tensor("wk", [D, 512], F32, kind="ExternalInput").ap()
    wv = nc.dram_tensor("wv", [D, 512], F32, kind="ExternalInput").ap()
    wo = nc.dram_tensor("wo", [D, D], F32, kind="ExternalInput").ap()
    bq_t = nc.dram_tensor("bq_t", [128, 16], F32, kind="ExternalInput").ap()
    bk_t = nc.dram_tensor("bk_t", [128, 4], F32, kind="ExternalInput").ap()
    bv_bc = nc.dram_tensor("bv_bc", [128, 512], F32, kind="ExternalInput").ap()
    bo_bc = nc.dram_tensor("bo_bc", [128, 2048], F32, kind="ExternalInput").ap()
    wg_d = nc.dram_tensor("wg8", [128, 44, 16, 128], U8, kind="ExternalInput").ap()
    wu_d = nc.dram_tensor("wu8", [128, 44, 16, 128], U8, kind="ExternalInput").ap()
    wd_d = nc.dram_tensor("wd8", [128, 8, 11, 4, 256], U8, kind="ExternalInput").ap()
    sg_t = nc.dram_tensor("sg_t", [128, 44], F32, kind="ExternalInput").ap()
    su_t = nc.dram_tensor("su_t", [128, 44], F32, kind="ExternalInput").ap()
    sd_bc = nc.dram_tensor("sd_bc", [128, 2048], F32, kind="ExternalInput").ap()
    kcos_in = nc.dram_tensor("kcos", [128, T], F32, kind="ExternalInput").ap()
    ksin_in = nc.dram_tensor("ksin", [128, T], F32, kind="ExternalInput").ap()
    qcos_in = nc.dram_tensor("qcos", [128, OWN], F32, kind="ExternalInput").ap()
    qsin_in = nc.dram_tensor("qsin", [128, OWN], F32, kind="ExternalInput").ap()
    masks_in = nc.dram_tensor("masks", [128, 512], BF16, kind="ExternalInput").ap()
    ident_in = nc.dram_tensor("ident", [128, 128], F32, kind="ExternalInput").ap()
    sel_in = nc.dram_tensor("sel", [128, 32], F32, kind="ExternalInput").ap()

    out = nc.dram_tensor("out", [OWN, D], F32, kind="ExternalOutput").ap()

    if dbg:
        dbg_p = nc.dram_tensor("dbg_p", [16, 128, 1024], BF16, kind="ExternalOutput").ap()
        dbg_av = nc.dram_tensor("dbg_av", [8, 128, 129], F32, kind="ExternalOutput").ap()
        dbg_q = nc.dram_tensor("dbg_q", [128, 8 * OWN], BF16, kind="ExternalOutput").ap()
        dbg_f2 = nc.dram_tensor("dbg_f2", [128, 8 * OWN], BF16, kind="ExternalOutput").ap()
        dbg_cc = nc.dram_tensor("dbg_cc", [128, 16], F32, kind="ExternalOutput").ap()
        dbg_gu = nc.dram_tensor("dbg_gu", [128, 44 * 1024], BF16, kind="ExternalOutput").ap()
        dbg_wg = nc.dram_tensor("dbg_wg", [128, 2048], BF16, kind="ExternalOutput").ap()
        dbg_xr = nc.dram_tensor("dbg_xr", [1, 1024], BF16, kind="ExternalOutput").ap()
        dbg_wd = nc.dram_tensor("dbg_wd", [128, 2112], BF16, kind="ExternalOutput").ap()
        dbg_d8 = nc.dram_tensor("dbg_d8", [128, 2048], U8, kind="ExternalOutput").ap()
    kt_scr = nc.dram_tensor("kt_scr", [4, 128, T], BF16, kind=SCR).ap()
    v_scr = nc.dram_tensor("v_scr", [4, 128, 64 * 129], BF16, kind=SCR).ap()
    xo_scr = nc.dram_tensor("xo_scr", [16, 128, OWN], BF16, kind=SCR).ap()
    ff_scr = nc.dram_tensor("ff_scr", [16, 128, D], F32, kind=SCR).ap()
    ft_scr = nc.dram_tensor("ft_scr", [8, 128, OWN], BF16, kind=SCR).ap()
    xo_pv = xo_scr.rearrange("a p c -> p a c")

    def at(name, off, shape, dt):
        return nc.alloc_sbuf_tensor_at(name, shape, dt, offset=off).ap()

    BASE = 16896
    A0, B0_, C0, F0 = BASE, BASE + 94208, BASE + 126976, BASE + 159744
    FP = F0

    def fp_(name, shape, dt, nbytes):
        nonlocal FP
        ap = at(name, FP, shape, dt)
        FP += (nbytes + 31) & ~31
        return ap

    masks = fp_("masks_s", [128, 512], BF16, 1024)
    bvb = fp_("bvb", [128, 512], F32, 2048)
    bqs = fp_("bqs", [128, 16], F32, 64)
    bks = fp_("bks", [128, 4], F32, 16)
    sgs = fp_("sgs", [128, 44], F32, 176)
    sus = fp_("sus", [128, 44], F32, 176)
    eps_c = fp_("eps_c", [128, 1], F32, 4)
    cb_c = fp_("cb_c", [1, 1], F32, 4)
    mk2 = fp_("mk2", [1, 4], F32, 16)
    tn = fp_("tn", [1, 8], F32, 32)
    tnb = fp_("tnb", [1, 2], BF16, 4)
    ccol = fp_("ccol", [128, 16], F32, 64)
    ident_f = fp_("ident_f", [128, 128], F32, 512)
    ident_bf = fp_("ident_bf", [128, 128], BF16, 256)
    sel_f = fp_("sel_f", [128, 32], F32, 128)
    sel_bf = fp_("sel_bf", [128, 32], BF16, 64)
    ones_bf = fp_("ones_bf", [128, 1], BF16, 4)
    negr_bf = fp_("negr_bf", [1, 128], BF16, 256)
    m128n = fp_("m128n", [1, 128], BF16, 256)
    ones1r = fp_("ones1r", [1, 128], BF16, 256)
    bvr_bf = fp_("bvr_bf", [1, 512], BF16, 1024)
    xrow = fp_("xrow", [1, 1024], F32, 4096)
    xrow_bf = fp_("xrow_bf", [1, 1024], BF16, 2048)
    rinv = fp_("rinv", [128, 8], F32, 32)
    s1c = [fp_(f"s1c{i}", [128, 1], F32, 4) for i in range(2)]
    s2c = [fp_(f"s2c{i}", [128, 1], F32, 4) for i in range(2)]
    s3c = [fp_(f"s3c{i}", [128, 1], F32, 4) for i in range(2)]
    s4c = [fp_(f"s4c{i}", [128, 1], F32, 4) for i in range(2)]
    mcell = [fp_(f"mcell{i}", [128, 1], F32, 4) for i in range(2)]
    vcell = [fp_(f"vcell{i}", [128, 1], F32, 4) for i in range(2)]
    tcell = [fp_(f"tcell{i}", [128, 1], F32, 4) for i in range(2)]
    rcell = [fp_(f"rcell{i}", [128, 1], F32, 4) for i in range(2)]
    FP = (FP + 63) & ~63
    FL = FP
    assert FL - F0 <= 13824, FL - F0

    xnb = [at(f"xnb{i}", A0 + 4096 * i, [128, D], BF16) for i in range(2)]
    wkv = at("wkv", A0 + 16384, [128, 16 * 1024], BF16)
    wkv3 = wkv.rearrange("p (a c) -> p a c", a=16)
    xot = [at(f"xot{i}", A0 + 49152 + 4096 * i, [128, 16 * 128], BF16) for i in range(2)]
    xot3 = [x.rearrange("p (a c) -> p a c", a=16) for x in xot]
    vst = [at(f"vst{i}", A0 + 57344 + 4160 * i, [128, 4 * 516], BF16) for i in range(2)]
    sqkb = at("sqkb", A0 + 65664, [128, 2048], BF16)
    xnT = at("xnT", B0_, [128, 16 * 512], BF16)
    xnT3 = xnT.rearrange("p (a c) -> p a c", a=16)
    ktst = [at(f"ktst{i}", B0_ + 16384 + 1024 * i, [128, 512], BF16) for i in range(2)]
    xb = [at(f"xb{i}", FL + 8192 * i, [128, D], F32) for i in range(2)]
    wkv_f = at("wkv_f", FL + 16384, [128, 1024], F32)
    rp_a = at("rp_a", FL + 20480, [128, 512], F32)
    rp_b = at("rp_b", FL + 22528, [128, 512], F32)
    rp_c = at("rp_c", FL + 24576, [128, 512], F32)
    tbl = [at(f"tbl{i}", FL + 26624 + 4096 * i, [128, 1024], F32) for i in range(2)]
    sqst = at("sqst", FL + 34816, [128, 1024], F32)

    guT = at("guT", A0, [128, 44 * 1024], BF16)
    gu3 = guT.rearrange("p (a c) -> p a c", a=44)
    xnf = at("xnf", B0_, [128, 16 * 1024], BF16)
    xnf3 = xnf.rearrange("p (a c) -> p a c", a=16)
    wgbx = [at(f"wgbx{i}", C0 + 4096 * i, [128, 2048], BF16) for i in range(2)]
    wubx = [at(f"wubx{i}", C0 + 8192 + 4096 * i, [128, 2048], BF16) for i in range(2)]
    wdbx = at("wdbx", C0 + 16384, [128, 2 * 4 * 264], BF16)
    wdbx4 = wdbx.rearrange("p (s a c) -> p s a c", s=2, a=4)
    wg8 = [at(f"wg8_{i}", C0 + 20608 + 2048 * i, [128, 2048], U8) for i in range(2)]
    wu8 = [at(f"wu8_{i}", C0 + 24704 + 2048 * i, [128, 2048], U8) for i in range(2)]
    wd8s = [at(f"wd8_{i}", C0 + 28800 + 1024 * i, [128, 1024], U8) for i in range(2)]
    wd8v = [w.rearrange("p (a c) -> p a c", a=4) for w in wd8s]
    fgo = [at(f"fgo{i}", FL + 4096 + 2048 * i, [128, 512], F32) for i in range(4)]
    dst = [at(f"dst{i}", FL + 16384 + 1024 * i, [128, 256], F32) for i in range(2)]
    sdb = at("sdb", FL + 18432, [128, 2048], F32)
    dcol = at("dcol", FL + 26624, [128, 8], F32)

    kth = at("kth", A0, [128, 2 * T], BF16)
    vh = at("vh", A0 + 32768, [128, 2 * 64 * 129], BF16)
    ptb = [at(f"ptb{i}", A0 + 65792 + 2048 * i, [128, 1024], BF16) for i in range(2)]
    fstg = [at(f"fstg{i}", A0 + 69888 + 2048 * i, [128, 1024], BF16) for i in range(2)]
    attb = at("attb", A0 + 73984, [128, 8 * 128], BF16)
    sqa = at("sqa", A0 + 76032, [128, 512], BF16)
    xnq = at("xnq", A0 + 77056, [128, 16 * 512], BF16)
    xnq3 = xnq.rearrange("p (a c) -> p a c", a=16)
    qt = at("qt", B0_, [128, 8 * OWN], BF16)
    wqb8 = at("wqb8", C0, [128, 16 * 1024], BF16)
    wqb83 = wqb8.rearrange("p (a c) -> p a c", a=16)
    featT2 = at("featT2", C0, [128, 8 * OWN], BF16)
    f23 = featT2.rearrange("p (a c) -> p a c", a=8)
    ra_a = at("ra_a", FL, [128, 512], F32)
    ra_b = at("ra_b", FL + 2048, [128, 512], F32)
    ra_c = at("ra_c", FL + 4096, [128, 512], F32)
    qrope = at("qrope", FL + 6144, [128, 2 * OWN], F32)
    wqf = at("wqf", FL + 22528, [128, 2048], F32)

    featT1 = at("featT1", A0, [128, 8 * OWN], BF16)
    f13 = featT1.rearrange("p (a c) -> p a c", a=8)
    wobp = [at(f"wob{i}", A0 + 32768 + 16384 * i, [128, 16 * 512], BF16)
            for i in range(2)]
    wob3p = [w.rearrange("p (a c) -> p a c", a=16) for w in wobp]
    wof = [at("wof0", FL, [128, 512], F32),
           at("wof1", FL + 22528, [128, 512], F32)]
    xo4 = [at(f"xo4_{i}", FL + 2048 + 2048 * i, [128, 512], F32) for i in range(2)]
    yf4 = [at(f"yf4_{i}", FL + 6144 + 2048 * i, [128, 512], F32) for i in range(2)]
    ost = [at(f"ost{i}", FL + 10240 + 2048 * i, [128, 512], F32) for i in range(2)]
    bob = at("bob", FL + 14336, [128, 2048], F32)

    P = nc.alloc_psum_tensor("P", [128, 4096], F32).ap()
    BK = [P[:, 512 * i: 512 * (i + 1)] for i in range(8)]
    AVOFF = [2048, 2177, 2306, 2560, 2689, 2818, 3072, 3201]
    AVF = [P[:, o:o + 129] for o in AVOFF]

    E = {"sync": [], "gpsimd": [], "vector": [], "scalar": [], "tensor": []}
    SEMS = []

    def mks(name):
        s = Sem(name)
        SEMS.append(s)
        return s

    sPE, sV, sS = mks("sPE"), mks("sV"), mks("sS")
    qC, qWKV, qSD, qXN, qKV, qKVg, qXQ, qP4a, qP4b, sDBG = (
        mks(n) for n in
        ["qC", "qWKV", "qSD", "qXN", "qKV", "qKVg", "qXQ", "qP4a", "qP4b", "sDBG"])
    qG2 = [mks(f"qG2{i}") for i in range(2)]
    qU = [mks(f"qU{i}") for i in range(2)]
    qW2 = [mks(f"qW2{i}") for i in range(2)]
    qTBL = [mks(f"qTBL{i}") for i in range(2)]
    qX = [mks(f"qX{i}") for i in range(2)]
    qQ = [mks(f"qQ{i}") for i in range(2)]
    qWO = [mks(f"qWO{i}") for i in range(2)]
    qXY = [mks(f"qXY{i}") for i in range(2)]
    sXOT = [mks(f"sXOT{i}") for i in range(2)]
    sKT = [mks(f"sKT{i}") for i in range(2)]
    sVS = [mks(f"sVS{i}") for i in range(2)]
    sDS = [mks(f"sDS{i}") for i in range(2)]
    sFS = [mks(f"sFS{i}") for i in range(2)]
    sOUT = [mks(f"sOUT{i}") for i in range(2)]
    E2S = {"vector": sV, "scalar": sS, "tensor": sPE}

    def wait(eng, sem, val):
        if val > 0:
            E[eng].append(lambda e, s=sem, v=val: e.wait_ge(s.h, v))

    def run(eng, meth, *a, **kw):
        E[eng].append(lambda e, m=meth, aa=a, kk=kw: getattr(e, m)(*aa, **kk))

    def runi(eng, meth, *a, **kw):
        sem = E2S[eng]
        E[eng].append(lambda e, m=meth, aa=a, kk=kw, s=sem:
                      getattr(e, m)(*aa, **kk).then_inc(s.h, 1))
        sem.cnt += 1
        return sem.cnt

    def dma(eng, sem, out_, in_):
        E[eng].append(lambda e, o=out_, i=in_, s=sem:
                      e.dma_start(out=o, in_=i).then_inc(s.h, 16))
        sem.cnt += 16
        return sem.cnt

    def mm(out_, lhsT, rhs, start, stop, inc=False, skip=False):
        if inc:
            E["tensor"].append(lambda e, o=out_, l=lhsT, r=rhs, st=start, sp=stop,
                               sk=skip:
                               e.matmul(out=o, lhsT=l, rhs=r, start=st, stop=sp,
                                        skip_group_check=sk).then_inc(sPE.h, 1))
            sPE.cnt += 1
            return sPE.cnt
        E["tensor"].append(lambda e, o=out_, l=lhsT, r=rhs, st=start, sp=stop,
                           sk=skip:
                           e.matmul(out=o, lhsT=l, rhs=r, start=st, stop=sp,
                                    skip_group_check=sk))
        return sPE.cnt

    def barrier():
        snap = [(s, s.cnt) for s in SEMS]
        for eng in E:
            for s, v in snap:
                if s is not E2S.get(eng):
                    wait(eng, s, v)

    bank_free = {b: (sV, 0) for b in range(8)}

    def pe_wait_bank(b):
        s, v = bank_free[b]
        wait("tensor", s, v)

    dma("sync", qC, ident_f[:], ident_in[:, :])
    dma("sync", qC, sel_f[:], sel_in[:, :])
    dma("sync", qC, masks[:], masks_in[:, :])
    dma("sync", qC, bqs[:], bq_t[:, :])
    dma("sync", qC, bks[:], bk_t[:, :])
    dma("sync", qC, sgs[:], sg_t[:, :])
    dma("sync", qC, sus[:], su_t[:, :])
    ldc = dma("sync", qC, bvb[:], bv_bc[:, :])
    wait("vector", qC, ldc)
    runi("vector", "tensor_copy", out=ident_bf[:], in_=ident_f[:])
    runi("vector", "tensor_copy", out=sel_bf[:], in_=sel_f[:])
    run("vector", "memset", ones_bf[:], 1.0)
    run("vector", "memset", negr_bf[:], -1.0)
    run("vector", "memset", m128n[:], -128.0)
    run("vector", "memset", ones1r[:], 1.0)
    runi("vector", "tensor_copy", out=bvr_bf[:], in_=bvb[0:1, :])
    run("vector", "memset", eps_c[:], LN_EPS)
    run("vector", "memset", cb_c[:], -0.5 * math.log(128.0))
    run("vector", "memset", mk2[:], 0.0)
    mz = 0
    for i in range(2):
        mz = runi("vector", "memset", vst[i][:], 0.0)
    wait("vector", sV, mz)
    for i in range(2):
        for g in range(4):
            for t4 in range(4):
                o = g * 516 + t4 * 129 + 128
                run("vector", "memset", vst[i][:, o:o + 1], 1.0)
    wk3 = wk.rearrange("(a p) f -> a p f", p=128)
    wv3 = wv.rearrange("(a p) f -> a p f", p=128)
    for dt in range(16):
        dma("sync", qWKV, wkv_f[:, 0:512], wk3[dt])
        ld = dma("sync", qWKV, wkv_f[:, 512:1024], wv3[dt])
        wait("vector", qWKV, ld)
        vdone = runi("vector", "tensor_copy", out=wkv3[:, dt, :], in_=wkv_f[:, 0:1024])
        if dt < 15:
            wait("sync", sV, vdone)

    xb_free = [0, 0]
    xb_free_s = [0, 0]
    sq_red = [0, 0]
    xnb_free = [0, 0]
    ktst_sp = [0, 0]
    ktsq = [0, 0]
    vst_sp = [0, 0]
    xot_sp = [0, 0]
    tbl_used = [0, 0]
    row_used = 0
    kv_last_mm = 0
    kpb_prev = 0
    kdone = [0, 0, 0, 0]

    for r in range(NCH):
        rb = r % 2
        wait("sync", sV, tbl_used[rb])
        dma("sync", qTBL[rb], tbl[rb][:, 0:512], kcos_in[:, r * 512:(r + 1) * 512])
        ldtb = dma("sync", qTBL[rb], tbl[rb][:, 512:1024],
                   ksin_in[:, r * 512:(r + 1) * 512])

        for jj in range(4):
            j = 4 * r + jj
            ib = j % 2
            wait("sync", sV, xb_free[ib])
            wait("sync", sS, xb_free_s[ib])
            ldx = dma("sync", qX[ib], xb[ib][:], x_in[j * 128:(j + 1) * 128, :])
            wait("vector", qX[ib], ldx)
            runi("vector", "reduce_sum", out=s1c[ib][:], in_=xb[ib][:], axis=AX.X)
            wait("scalar", qX[ib], ldx)
            cells = [s2c[ib], tcell[ib], s3c[ib], s4c[ib]]
            red4 = 0
            for q4 in range(4):
                half = (q4 % 2) * 512
                wait("scalar", sV, sq_red[q4 % 2])
                sq4 = runi("scalar", "activation", out=sqst[:, half:half + 512],
                           in_=xb[ib][:, q4 * 512:(q4 + 1) * 512], func=AF.Square)
                wait("vector", sS, sq4)
                red4 = runi("vector", "reduce_sum", out=cells[q4][:],
                            in_=sqst[:, half:half + 512], axis=AX.X)
                sq_red[q4 % 2] = red4
            xb_free_s[ib] = sq4
            wait("vector", sV, red4)
            run("vector", "tensor_scalar", out=mcell[ib][:], in0=s1c[ib][:],
                scalar1=1.0 / D, scalar2=None, op0=ALU.mult)
            run("vector", "tensor_tensor", out=s2c[ib][:], in0=s2c[ib][:],
                in1=tcell[ib][:], op=ALU.add)
            anc = runi("vector", "tensor_tensor", out=s3c[ib][:], in0=s3c[ib][:],
                       in1=s4c[ib][:], op=ALU.add)
            wait("vector", sV, anc)
            run("vector", "tensor_tensor", out=s2c[ib][:], in0=s2c[ib][:],
                in1=s3c[ib][:], op=ALU.add)
            anc = runi("vector", "tensor_tensor", out=tcell[ib][:], in0=mcell[ib][:],
                       in1=s1c[ib][:], op=ALU.mult)
            wait("vector", sV, anc)
            vv = runi("vector", "tensor_tensor", out=vcell[ib][:], in0=s2c[ib][:],
                      in1=tcell[ib][:], op=ALU.subtract)
            wait("scalar", sV, vv)
            rr0 = runi("scalar", "activation", out=rcell[ib][:], in_=vcell[ib][:],
                       func=AF.Ln, bias=eps_c[:], scale=1.0 / D)
            wait("scalar", sS, rr0)
            rr = runi("scalar", "activation", out=rcell[ib][:], in_=rcell[ib][:],
                      func=AF.Exp, scale=-0.5)
            wait("vector", sS, rr)
            wait("vector", sPE, xnb_free[ib])
            ap = runi("vector", "tensor_scalar", out=xnb[ib][:], in0=xb[ib][:],
                      scalar1=mcell[ib][:], scalar2=rcell[ib][:],
                      op0=ALU.subtract, op1=ALU.mult)
            xb_free[ib] = ap

            wait("tensor", sV, ap)
            for dtg in range(4):
                tb = (j * 4 + dtg) % 2
                pe_wait_bank(tb)
                tmm = 0
                for k in range(4):
                    dt = dtg * 4 + k
                    tmm = mm(BK[tb][:, k * 128:(k + 1) * 128],
                             xnb[ib][:, dt * 128:(dt + 1) * 128], ident_bf[:],
                             start=(k == 0), stop=(k == 3), inc=(k == 3))
                wait("scalar", sPE, tmm)
                if r > 0 and jj == 0 and dtg == 0:
                    wait("scalar", sPE, kv_last_mm)
                ev = runi("scalar", "copy",
                          out=xnT3[:, dtg * 4:(dtg + 1) * 4, jj * 128:(jj + 1) * 128],
                          in_=BK[tb].rearrange("p (a c) -> p a c", a=4))
                bank_free[tb] = (sS, ev)
            pe_wait_bank(6)
            smm = 0
            for dt in range(16):
                smm = mm(BK[6][:, dt * 32:(dt + 1) * 32],
                         xnb[ib][:, dt * 128:(dt + 1) * 128], sel_bf[:],
                         start=(dt == 0), stop=(dt == 15), inc=(dt == 15))
            xnb_free[ib] = smm
            wait("scalar", sPE, smm)
            if jj == 0:
                wait("scalar", sXOT[rb], xot_sp[rb])
            sev = runi("scalar", "copy",
                       out=xot3[rb][:, :, jj * 32:(jj + 1) * 32],
                       in_=BK[6].rearrange("p (a c) -> p a c", a=16))
            bank_free[6] = (sS, sev)
        xnT_evac = sS.cnt

        wait("gpsimd", sS, xnT_evac)
        xot_sp[rb] = dma("gpsimd", sXOT[rb], xo_pv[:, :, r * 128:(r + 1) * 128],
                         xot3[rb][:, :, :])

        wait("tensor", sS, xnT_evac)
        for g in range(4):
            kb = 2 + (g % 2)
            pe_wait_bank(kb)
            kmm = 0
            for dt in range(16):
                kmm = mm(BK[kb], wkv3[:, dt, g * 128:(g + 1) * 128], xnT3[:, dt, :],
                         start=(dt == 0), stop=(dt == 15), inc=(dt == 15))
            wait("scalar", sPE, kmm)
            wait("scalar", sV, kpb_prev)
            pa = runi("scalar", "activation", out=rp_a[:], in_=BK[kb],
                      func=AF.Identity, bias=bks[:, g:g + 1])
            bank_free[kb] = (sS, pa)
            wait("scalar", sS, pa)
            run("scalar", "copy", out=rp_b[0:64, :], in_=rp_a[64:128, :])
            pb = runi("scalar", "copy", out=rp_b[64:128, :], in_=rp_a[0:64, :])
            wait("vector", qTBL[rb], ldtb)
            wait("vector", sS, pb)
            wait("vector", sKT[g % 2], ktst_sp[g % 2])
            wait("vector", sS, ktsq[g % 2])
            run("vector", "tensor_tensor", out=rp_c[:], in0=rp_a[:],
                in1=tbl[rb][:, 0:512], op=ALU.mult)
            anc = runi("vector", "tensor_tensor", out=rp_b[:], in0=rp_b[:],
                       in1=tbl[rb][:, 512:1024], op=ALU.mult)
            kpb_prev = anc
            wait("vector", sV, anc)
            kdone[g] = runi("vector", "tensor_tensor", out=ktst[g % 2][:],
                            in0=rp_c[:], in1=rp_b[:], op=ALU.add)
            wait("gpsimd", sV, kdone[g])
            ktst_sp[g % 2] = dma("gpsimd", sKT[g % 2],
                                 kt_scr[g][:, r * 512:(r + 1) * 512],
                                 ktst[g % 2][:])
            wait("scalar", sV, kdone[g])
            ktsq[g % 2] = runi("scalar", "activation",
                               out=sqkb[:, g * 512:(g + 1) * 512],
                               in_=ktst[g % 2][:], func=AF.Square)
        tbl_used[rb] = sV.cnt
        sq_all = sS.cnt

        for t4 in range(4):
            vb = 4 + (t4 % 2)
            pe_wait_bank(vb)
            for dt in range(16):
                mm(BK[vb], xnT3[:, dt, t4 * 128:(t4 + 1) * 128],
                   wkv3[:, dt, 512:1024], start=(dt == 0), stop=False)
            vmm = mm(BK[vb], ones1r[:], bvr_bf[:], start=False, stop=True, inc=True)
            wait("scalar", sPE, vmm)
            if t4 == 0:
                wait("scalar", sVS[rb], vst_sp[rb])
            vev = 0
            for g in range(4):
                vev = runi("scalar", "copy",
                           out=vst[rb][:, g * 516 + t4 * 129: g * 516 + t4 * 129 + 128],
                           in_=BK[vb][:, g * 128:(g + 1) * 128])
            bank_free[vb] = (sS, vev)
        kv_last_mm = sPE.cnt
        wait("gpsimd", sS, sS.cnt)
        for g in range(4):
            vst_sp[rb] = dma("gpsimd", sVS[rb], v_scr[g][:, r * 516:(r + 1) * 516],
                             vst[rb][:, g * 516:(g + 1) * 516])

        for half in range(2):
            g0, g1 = 2 * half, 2 * half + 1
            wait("tensor", sS, sq_all)
            wait("tensor", sV, row_used)
            if half == 0:
                s_, v_ = bank_free[6]
                wait("tensor", s_, v_)
            mm(P[0:1, 3072:3584], ones_bf[:], sqkb[:, g0 * 512:(g0 + 1) * 512],
               start=True, stop=True)
            rs = mm(P[0:1, 3584:4096], ones_bf[:], sqkb[:, g1 * 512:(g1 + 1) * 512],
                    start=True, stop=True, inc=True)
            wait("vector", sPE, rs)
            run("vector", "reduce_max", out=tn[0:1, 0:1],
                in_=P[0:1, 3072:3584], axis=AX.X)
            rmx = runi("vector", "reduce_max", out=tn[0:1, 1:2],
                       in_=P[0:1, 3584:4096], axis=AX.X)
            wait("vector", sV, rmx)
            run("vector", "tensor_tensor", out=mk2[0:1, g0:g0 + 1],
                in0=mk2[0:1, g0:g0 + 1], in1=tn[0:1, 0:1], op=ALU.max)
            row_used = runi("vector", "tensor_tensor", out=mk2[0:1, g1:g1 + 1],
                            in0=mk2[0:1, g1:g1 + 1], in1=tn[0:1, 1:2], op=ALU.max)
            bank_free[6] = (sV, row_used)

    barrier()
    ldsd = dma("sync", qSD, sdb[:], sd_bc[:, :])
    wmset = runi("vector", "memset", wdbx[:], 128.0)
    dst_sp = [0, 0]
    g8_free = [0, 0]
    u8_free = [0, 0]
    d8_free = [0, 0]
    gmm_ht = [0, 0]
    dmm_sl = [0, 0]
    gq_par = [0, 0]

    gate_mm_last = 0
    for F in range(2):
        if F > 0:
            wait("sync", sPE, gate_mm_last)
        ldxn = dma("sync", qXN, xnf3[:, :, :], xo_pv[:, :, F * 1024:(F + 1) * 1024])
        wait("tensor", qXN, ldxn)
        xcp = 0
        for tc in range(2):
            wait("tensor", sV, xcp)
            if tc == 0:
                s_, v_ = bank_free[7]
                wait("tensor", s_, v_)
            xr = 0
            for dt in range(16):
                xr = mm(P[0:1, 3584:4096], ones_bf[:],
                        xnf3[:, dt, tc * 512:(tc + 1) * 512],
                        start=(dt == 0), stop=(dt == 15), inc=(dt == 15))
            wait("vector", sPE, xr)
            xcp = runi("vector", "tensor_copy", out=xrow[0:1, tc * 512:(tc + 1) * 512],
                       in_=P[0:1, 3584:4096])
        wait("vector", sV, xcp)
        xrc = runi("vector", "tensor_copy", out=xrow_bf[:], in_=xrow[:])
        bank_free[7] = (sV, xcp)

        for ht in range(NHT):
            pp = ht % 2
            wait("sync", sV, g8_free[pp])
            ldg = dma("sync", qG2[pp], wg8[pp][:], wg_d[:, ht])
            wait("sync", sV, u8_free[pp])
            ldu = dma("sync", qU[pp], wu8[pp][:], wu_d[:, ht])
            wait("vector", qG2[pp], ldg)
            wait("vector", sPE, gmm_ht[pp])
            g8_free[pp] = runi("vector", "tensor_copy", out=wgbx[pp][:],
                               in_=wg8[pp][:])
            wait("vector", qU[pp], ldu)
            ucst = runi("vector", "tensor_copy", out=wubx[pp][:], in_=wu8[pp][:])
            u8_free[pp] = ucst
            wait("tensor", sV, ucst)
            if ht == 0:
                wait("tensor", sV, xrc)
            for tc in range(2):
                gb, ub = tc, 2 + tc
                pe_wait_bank(gb)
                for dt in range(16):
                    mm(BK[gb], wgbx[pp][:, dt * 128:(dt + 1) * 128],
                       xnf3[:, dt, tc * 512:(tc + 1) * 512],
                       start=(dt == 0), stop=False)
                gmm = mm(BK[gb], m128n[:], xrow_bf[0:1, tc * 512:(tc + 1) * 512],
                         start=False, stop=True, inc=True)
                pe_wait_bank(ub)
                for dt in range(16):
                    mm(BK[ub], wubx[pp][:, dt * 128:(dt + 1) * 128],
                       xnf3[:, dt, tc * 512:(tc + 1) * 512],
                       start=(dt == 0), stop=False)
                umm = mm(BK[ub], m128n[:], xrow_bf[0:1, tc * 512:(tc + 1) * 512],
                         start=False, stop=True, inc=True)
                gmm_ht[pp] = umm
                gate_mm_last = umm
                fg, fu = fgo[2 * (tc % 2)], fgo[2 * (tc % 2) + 1]
                wait("scalar", sPE, gmm)
                wait("scalar", sV, gq_par[tc % 2])
                sg1 = runi("scalar", "activation", out=fg[:], in_=BK[gb],
                           func=AF.Silu, scale=sgs[:, ht:ht + 1])
                bank_free[gb] = (sS, sg1)
                wait("scalar", sPE, umm)
                sact = runi("scalar", "mul", out=fu[:], in_=BK[ub],
                            mul=sus[:, ht:ht + 1])
                bank_free[ub] = (sS, sact)
                wait("vector", sS, sact)
                gq_par[tc % 2] = runi("vector", "tensor_tensor",
                                      out=gu3[:, ht, tc * 512:(tc + 1) * 512],
                                      in0=fg[:], in1=fu[:], op=ALU.mult)
        gu_done = sV.cnt
        if dbg and F == 1:
            wait("gpsimd", sV, sV.cnt)
            dma("gpsimd", sDBG, dbg_gu[:, :], guT[:, :])
            dma("gpsimd", sDBG, dbg_wg[:, :], wgbx[1][:])
            dma("gpsimd", sDBG, dbg_xr[:, :], xrow_bf[:, :])

        DT = [P[:, 512 * b: 512 * b + 257] for b in range(8)]
        wait("tensor", sV, gu_done)
        if F == 0:
            wait("vector", qSD, ldsd)
        stopc = [0] * 8

        def plan_evac(dcp):
            for tt in range(8):
                wait("scalar", sPE, stopc[tt])
                cpy = runi("scalar", "copy", out=dcol[:, tt:tt + 1],
                           in_=DT[tt][:, 256:257])
                wait("vector", sDS[tt % 2], dst_sp[tt % 2])
                wait("vector", sS, cpy)
                dv = runi("vector", "scalar_tensor_tensor", out=dst[tt % 2][:],
                          in0=DT[tt][:, 0:256], scalar=dcol[:, tt:tt + 1],
                          in1=sdb[:, dcp * 256:(dcp + 1) * 256],
                          op0=ALU.subtract, op1=ALU.mult)
                bank_free[tt] = (sV, dv)
                wait("gpsimd", sV, dv)
                dst_sp[tt % 2] = dma("gpsimd", sDS[tt % 2],
                                     ff_scr[F * 8 + tt][:, dcp * 256:(dcp + 1) * 256],
                                     dst[tt % 2][:])

        for dc in range(8):
            for htg in range(11):
                sl = (dc * 11 + htg) % 2
                wait("sync", sV, d8_free[sl])
                ldw = dma("sync", qW2[sl], wd8s[sl][:], wd_d[:, dc, htg])
                wait("vector", qW2[sl], ldw)
                wait("vector", sPE, dmm_sl[sl])
                d8_free[sl] = runi("vector", "tensor_copy",
                                   out=wdbx4[:, sl, :, 0:256], in_=wd8v[sl][:, :, :])
                if htg == 0 and dc > 0:
                    plan_evac(dc - 1)
                wait("tensor", sV, d8_free[sl])
                dmm = 0
                for tt in range(8):
                    if htg == 0:
                        s_, v_ = bank_free[tt]
                        wait("tensor", s_, v_)
                    for a in range(4):
                        dmm = mm(DT[tt], gu3[:, htg * 4 + a, tt * 128:(tt + 1) * 128],
                                 wdbx4[:, sl, a, 0:257],
                                 start=(htg == 0 and a == 0),
                                 stop=(htg == 10 and a == 3),
                                 inc=(a == 3 and (tt == 7 or htg == 10)))
                    if htg == 10:
                        stopc[tt] = dmm
                dmm_sl[sl] = dmm
        plan_evac(7)
        if dbg and F == 1:
            wait("gpsimd", sV, sV.cnt)
            dma("gpsimd", sDBG, dbg_wd[:, :], wdbx[:, :])
            dma("gpsimd", sDBG, dbg_d8[:, 0:1024], wd8s[0][:, :])
            dma("gpsimd", sDBG, dbg_d8[:, 1024:2048], wd8s[1][:, :])

    if phases == "all":
        qr = [0, 0]
        for H in range(2):
            barrier()
            if H == 0:
                dma("scalar", qKV, qrope[:, 0:OWN], qcos_in[:, :])
                dma("scalar", qKV, qrope[:, OWN:2 * OWN], qsin_in[:, :])
            rope_cnt = qKV.cnt
            ldxq0 = dma("sync", qXQ, xnq3[:, :, :], xo_pv[:, :, 0:512])
            for gl in range(2):
                dma("scalar", qKVg, kth[:, gl * T:(gl + 1) * T],
                    kt_scr[2 * H + gl][:, :])
                dma("scalar", qKVg, vh[:, gl * 8256:(gl + 1) * 8256],
                    v_scr[2 * H + gl][:, :])
            kvg_cnt = qKVg.cnt
            qr_dt = [0] * 16
            for dt in range(16):
                wait("sync", sV, qr[dt % 2])
                ldq = dma("sync", qQ[dt % 2],
                          wqf[:, (dt % 2) * 1024:(dt % 2) * 1024 + 1024],
                          wq[dt * 128:(dt + 1) * 128, H * 1024:(H + 1) * 1024])
                wait("vector", qQ[dt % 2], ldq)
                qr[dt % 2] = runi("vector", "tensor_copy", out=wqb83[:, dt, :],
                                  in_=wqf[:, (dt % 2) * 1024:(dt % 2) * 1024 + 1024])
                qr_dt[dt] = qr[dt % 2]

            qdone_prev = 0
            qpb_prev = 0
            qmm_last = 0
            for qch in range(4):
                if qch == 0:
                    ldxq = ldxq0
                else:
                    wait("sync", sPE, qmm_last)
                    ldxq = dma("sync", qXQ, xnq3[:, :, :],
                               xo_pv[:, :, qch * 512:(qch + 1) * 512])
                wait("tensor", qXQ, ldxq)
                for hl in range(8):
                    h = 8 * H + hl
                    qb = hl % 2
                    pe_wait_bank(qb)
                    qmm = 0
                    for dt in range(16):
                        if qch == 0 and hl == 0:
                            wait("tensor", sV, qr_dt[dt])
                        qmm = mm(BK[qb], wqb83[:, dt, hl * 128:(hl + 1) * 128],
                                 xnq3[:, dt, :], start=(dt == 0), stop=(dt == 15),
                                 inc=(dt == 15))
                    qmm_last = qmm
                    wait("vector", sPE, qmm)
                    wait("vector", sS, qpb_prev)
                    pbv = runi("vector", "tensor_scalar", out=ra_a[:], in0=BK[qb],
                               scalar1=bqs[:, h:h + 1], scalar2=None, op0=ALU.add)
                    bank_free[qb] = (sV, pbv)
                    wait("scalar", sV, pbv)
                    run("scalar", "copy", out=ra_b[0:64, :], in_=ra_a[64:128, :])
                    pb = runi("scalar", "copy", out=ra_b[64:128, :], in_=ra_a[0:64, :])
                    qpb_prev = pb
                    wait("vector", sS, pb)
                    if H == 0 and qch == 0 and hl == 0:
                        wait("vector", qKV, rope_cnt)
                    run("vector", "tensor_tensor", out=ra_c[:], in0=ra_a[:],
                        in1=qrope[:, qch * 512:(qch + 1) * 512], op=ALU.mult)
                    anc = runi("vector", "tensor_tensor", out=ra_b[:], in0=ra_b[:],
                               in1=qrope[:, OWN + qch * 512: OWN + (qch + 1) * 512],
                               op=ALU.mult)
                    wait("vector", sV, anc)
                    qs = qt[:, hl * OWN + qch * 512: hl * OWN + (qch + 1) * 512]
                    qdone_prev = runi("vector", "tensor_tensor", out=qs, in0=ra_c[:],
                                      in1=ra_b[:], op=ALU.add)
                    wait("vector", sV, qdone_prev)
                    sqv = runi("vector", "tensor_tensor", out=sqa[:], in0=qs, in1=qs,
                               op=ALU.mult)
                    wait("tensor", sV, sqv)
                    wait("tensor", sV, row_used)
                    rmm = mm(P[0:1, 3584:4096], ones_bf[:], sqa[:],
                             start=True, stop=True, inc=True)
                    wait("vector", sPE, rmm)
                    mq = runi("vector", "reduce_max", out=tn[0:1, hl % 2: hl % 2 + 1],
                              in_=P[0:1, 3584:4096], axis=AX.X)
                    row_used = mq
                    if hl % 2 == 1:
                        wait("vector", sV, mq)
                        anc = runi("vector", "tensor_tensor", out=tn[0:1, 2:3],
                                   in0=tn[0:1, 0:1], in1=tn[0:1, 1:2], op=ALU.max)
                        wait("vector", sV, anc)
                        g = hl // 4
                        mx = runi("vector", "tensor_tensor", out=tn[0:1, 2:3],
                                  in0=tn[0:1, 2:3], in1=mk2[0:1, 2 * H + g: 2 * H + g + 1],
                                  op=ALU.mult)
                        wait("scalar", sV, mx)
                        lg = runi("scalar", "activation", out=tn[0:1, 3:4],
                                  in_=tn[0:1, 2:3], func=AF.Ln)
                        wait("scalar", sS, lg)
                        cc = runi("scalar", "activation", out=tn[0:1, 4:5],
                                  in_=tn[0:1, 3:4], func=AF.Exp, scale=0.5, bias=cb_c[:])
                        wait("vector", sS, cc)
                        cbf = runi("vector", "tensor_copy", out=tnb[0:1, 0:1],
                                   in_=tn[0:1, 4:5])
                        wait("tensor", sV, cbf)
                        cmm = mm(P[:, 3584:3585], negr_bf[:], tnb[0:1, 0:1],
                                 start=True, stop=True, inc=True)
                        wait("vector", sPE, cmm)
                        ci = (hl // 2) * 4 + qch
                        row_used = runi("vector", "tensor_copy",
                                        out=ccol[:, ci:ci + 1], in_=P[:, 3584:3585])
            ccol_done = sV.cnt
            wait("scalar", sV, ccol_done)
            wait("tensor", sV, ccol_done)
            wait("tensor", qKVg, kvg_cnt)
            if dbg and H == dbgH:
                wait("gpsimd", sV, ccol_done)
                dma("gpsimd", sDBG, dbg_q[:, :], qt[:, :])

            sbank_free = [0, 0]
            pt_free = [0, 0]
            dbg_dma = [0, 0]
            av_free = 0
            fstg_sp = [0, 0]
            pend_fin = [None]
            b7_free = ccol_done
            for qch in range(4):
                kts = 16 * qch + 16
                for pr in range(4):
                    hl0, hl1 = 2 * pr, 2 * pr + 1
                    gl = pr // 2
                    def plan_S(kt):
                        e = kt - 16 * qch
                        col0 = 128 * (0 if e < 0 else e // 4)
                        soff = 1024 * (kt % 2)
                        wait("tensor", sS, sbank_free[kt % 2])
                        last = 0
                        for hh in range(2):
                            hl = (hl0, hl1)[hh]
                            last = mm(P[:, soff + hh * 512 + col0: soff + (hh + 1) * 512],
                                      kth[:, gl * T + kt * 128: gl * T + (kt + 1) * 128],
                                      qt[:, hl * OWN + qch * 512 + col0: hl * OWN + (qch + 1) * 512],
                                      start=True, stop=True, inc=(hh == 1))
                        return last

                    avc = 0
                    lastS = plan_S(0)
                    if pend_fin[0] is not None:
                        pend_fin[0]()
                        pend_fin[0] = None
                    for kt in range(kts):
                        e = kt - 16 * qch
                        m = 0 if e < 0 else e // 4
                        col0 = 128 * m
                        sb_i = kt % 2
                        soff = 1024 * sb_i
                        last = lastS
                        if kt + 1 < kts:
                            lastS = plan_S(kt + 1)
                        mk = 0
                        if 0 <= e < 16:
                            qi, dd = e // 4, e % 4
                            wait("vector", sPE, last)
                            for hh in range(2):
                                o = soff + hh * 512 + qi * 128
                                mk = runi("vector", "tensor_tensor", out=P[:, o:o + 128],
                                          in0=P[:, o:o + 128],
                                          in1=masks[:, dd * 128:(dd + 1) * 128], op=ALU.add)
                        wait("scalar", sPE, last)
                        if mk:
                            wait("scalar", sV, mk)
                        if kt >= 2:
                            wait("scalar", sPE, pt_free[sb_i])
                            if dbg:
                                wait("scalar", sDBG, dbg_dma[sb_i])
                        ci = pr * 4 + qch
                        if m == 0:
                            ex = runi("scalar", "activation", out=ptb[sb_i][:, 0:1024],
                                      in_=P[:, soff:soff + 1024], func=AF.Exp,
                                      scale=EXPS, bias=ccol[:, ci:ci + 1])
                        else:
                            for hh in range(2):
                                ex = runi("scalar", "activation",
                                          out=ptb[sb_i][:, hh * 512 + col0:(hh + 1) * 512],
                                          in_=P[:, soff + hh * 512 + col0: soff + (hh + 1) * 512],
                                          func=AF.Exp, scale=EXPS, bias=ccol[:, ci:ci + 1])
                        sbank_free[sb_i] = ex
                        if dbg and H == dbgH and qch == 0 and pr == 0:
                            wait("gpsimd", sS, ex)
                            dbg_dma[sb_i] = dma("gpsimd", sDBG, dbg_p[kt][:, :],
                                                ptb[sb_i][:, :])
                        if kt == 0:
                            wait("tensor", sV, av_free)
                        wait("tensor", sS, ex)
                        for hh in range(2):
                            for qi in range(m, 4):
                                sl = hh * 4 + qi
                                avc = mm(AVF[sl],
                                         ptb[sb_i][:, hh * 512 + qi * 128: hh * 512 + (qi + 1) * 128],
                                         vh[:, gl * 8256 + kt * 129: gl * 8256 + (kt + 1) * 129],
                                         start=(kt == 0 and sl in (0, 3, 6)),
                                         stop=(kt == 16 * qch + 4 * qi + 3),
                                         inc=(hh == 1 and qi == 3), skip=True)
                        pt_free[sb_i] = avc
                    wait("vector", sPE, avc)
                    if dbg and H == dbgH and qch == 0 and pr == 0:
                        for sl in range(8):
                            dcp = runi("vector", "tensor_copy", out=ra_c[:, 0:129],
                                       in_=AVF[sl])
                            wait("gpsimd", sV, dcp)
                            dga = dma("gpsimd", sDBG, dbg_av[sl][:, :], ra_c[:, 0:129])
                            wait("vector", sDBG, dga)
                    rc = 0
                    for sl in range(8):
                        rc = runi("vector", "reciprocal", out=rinv[:, sl:sl + 1],
                                  in_=AVF[sl][:, 128:129])
                    wait("vector", sV, rc)
                    fz = 0
                    for sl in range(8):
                        fz = runi("vector", "tensor_scalar",
                                  out=attb[:, sl * 128:(sl + 1) * 128],
                                  in0=AVF[sl][:, 0:128], scalar1=rinv[:, sl:sl + 1],
                                  scalar2=None, op0=ALU.mult)
                    av_free = fz

                    def fin(qch=qch, pr=pr, hl0=hl0, hl1=hl1, fz=fz):
                        nonlocal b7_free
                        for hh in range(2):
                            wait("tensor", sV, b7_free)
                            wait("tensor", sV, fz)
                            ft = 0
                            for k in range(4):
                                sl = hh * 4 + k
                                ft = mm(P[:, 3584 + k * 128: 3584 + (k + 1) * 128],
                                        attb[:, sl * 128:(sl + 1) * 128], ident_bf[:],
                                        start=(k == 0), stop=(k == 3), inc=(k == 3))
                            wait("vector", sPE, ft)
                            hl = (hl0, hl1)[hh]
                            if H == 0:
                                if hh == 0:
                                    wait("vector", sFS[pr % 2], fstg_sp[pr % 2])
                                fe = runi("vector", "tensor_copy",
                                          out=fstg[pr % 2][:, hh * 512:(hh + 1) * 512],
                                          in_=P[:, 3584:4096])
                            else:
                                fe = runi("vector", "tensor_copy",
                                          out=f23[:, hl, qch * 512:(qch + 1) * 512],
                                          in_=P[:, 3584:4096])
                            b7_free = fe
                        if H == 0:
                            wait("gpsimd", sV, sV.cnt)
                            for hh in range(2):
                                hl = (hl0, hl1)[hh]
                                fstg_sp[pr % 2] = dma("gpsimd", sFS[pr % 2],
                                                      ft_scr[hl][:, qch * 512:(qch + 1) * 512],
                                                      fstg[pr % 2][:, hh * 512:(hh + 1) * 512])
                    pend_fin[0] = fin
            pend_fin[0]()
            pend_fin[0] = None
            row_used = sV.cnt

        if dbg:
            wait("gpsimd", sV, sV.cnt)
            dma("gpsimd", sDBG, dbg_f2[:, :], featT2[:, :])
            dma("gpsimd", sDBG, dbg_cc[:, :], ccol[:, :])

        barrier()
        dma("sync", qP4a, bob[:], bo_bc[:, :])
        for ft in range(8):
            if ft % 2 == 0:
                dma("sync", qP4a, f13[:, ft, :], ft_scr[ft][:, :])
            else:
                dma("scalar", qP4b, f13[:, ft, :], ft_scr[ft][:, :])
        wob_c = [0, 0]
        wob_mm = [0, 0]
        st_last = [0, 0]
        ring_free = [0, 0]
        for dc in range(4):
            wb = dc % 2
            for ft in range(16):
                wait("sync", sV, wob_c[ft % 2])
                ldw = dma("sync", qWO[ft % 2], wof[ft % 2][:, :],
                          wo[ft * 128:(ft + 1) * 128, dc * 512:(dc + 1) * 512])
                wait("vector", qWO[ft % 2], ldw)
                if ft == 0:
                    wait("vector", sPE, wob_mm[wb])
                wob_c[ft % 2] = runi("vector", "tensor_copy",
                                     out=wob3p[wb][:, ft, :], in_=wof[ft % 2][:])
            wcast = sV.cnt
            wait("tensor", sV, wcast)
            if dc == 0:
                wait("tensor", qP4a, qP4a.cnt)
                wait("tensor", qP4b, qP4b.cnt)
            for tt in range(16):
                wait("scalar", sV, ring_free[tt % 2])
                dma("scalar", qXY[tt % 2], xo4[tt % 2][:],
                    x_own[tt * 128:(tt + 1) * 128, dc * 512:(dc + 1) * 512])
                ldy = dma("scalar", qXY[tt % 2], yf4[tt % 2][:],
                          ff_scr[tt][:, dc * 512:(dc + 1) * 512])
                ob = tt % 2
                pe_wait_bank(ob)
                omm = 0
                for ft in range(16):
                    f3 = f13 if ft < 8 else f23
                    omm = mm(BK[ob], f3[:, ft % 8, tt * 128:(tt + 1) * 128],
                             wob3p[wb][:, ft, :], start=(ft == 0), stop=(ft == 15),
                             inc=(ft == 15))
                wob_mm[wb] = omm
                wait("vector", sPE, omm)
                wait("vector", qXY[tt % 2], ldy)
                wait("vector", sOUT[tt % 2], st_last[tt % 2])
                run("vector", "tensor_tensor", out=ost[tt % 2][:], in0=BK[ob],
                    in1=bob[:, dc * 512:(dc + 1) * 512], op=ALU.add)
                anc = runi("vector", "tensor_tensor", out=xo4[tt % 2][:],
                           in0=xo4[tt % 2][:], in1=yf4[tt % 2][:], op=ALU.add)
                wait("vector", sV, anc)
                oe = runi("vector", "tensor_tensor", out=ost[tt % 2][:],
                          in0=ost[tt % 2][:], in1=xo4[tt % 2][:], op=ALU.add)
                bank_free[ob] = (sV, oe)
                ring_free[tt % 2] = oe
                wait("gpsimd", sV, oe)
                st_last[tt % 2] = dma("gpsimd", sOUT[tt % 2],
                                      out[tt * 128:(tt + 1) * 128,
                                          dc * 512:(dc + 1) * 512],
                                      ost[tt % 2][:])
        for e_ in ("gpsimd", "sync"):
            wait(e_, sOUT[0], sOUT[0].cnt)
            wait(e_, sOUT[1], sOUT[1].cnt)
            if dbg:
                wait(e_, sDBG, sDBG.cnt)

    with contextlib.ExitStack() as stack:
        for s_ in SEMS:
            s_.h = stack.enter_context(nc.semaphore(s_.name))
        blk = stack.enter_context(nc.Block())

        @blk.sync
        def _(e):
            for f_ in E["sync"]:
                f_(e)

        @blk.gpsimd
        def _(e):
            for f_ in E["gpsimd"]:
                f_(e)

        @blk.vector
        def _(e):
            for f_ in E["vector"]:
                f_(e)

        @blk.scalar
        def _(e):
            for f_ in E["scalar"]:
                f_(e)

        @blk.tensor
        def _(e):
            for f_ in E["tensor"]:
                f_(e)

    return nc



_ROPE_BASE = 1000000.0


def _rope_tables(positions):
    inv = 1.0 / _ROPE_BASE ** (np.arange(0, 64, dtype=np.float64) * 2.0 / 128.0)
    ang = inv[:, None] * np.asarray(positions, dtype=np.float64)[None, :]
    cos = np.cos(ang)
    sin = np.sin(ang)
    kcos = np.ascontiguousarray(np.concatenate([cos, cos], axis=0).astype(np.float32))
    ksin = np.ascontiguousarray(np.concatenate([-sin, sin], axis=0).astype(np.float32))
    return kcos, ksin


def host_inmaps(L0_attn_x, L0_attn_Wq, L0_attn_bq, L0_attn_Wk, L0_attn_bk,
                L0_attn_Wv, L0_attn_bv, L0_attn_Wo, L0_attn_bo,
                L0_ffn_wq_gate, L0_ffn_s_gate, L0_ffn_wq_up, L0_ffn_s_up,
                L0_ffn_wq_down, L0_ffn_s_down):
    x = np.asarray(L0_attn_x, dtype=np.float32)
    wq = np.ascontiguousarray(np.asarray(L0_attn_Wq, dtype=np.float32))
    wk = np.ascontiguousarray(np.asarray(L0_attn_Wk, dtype=np.float32))
    wv = np.ascontiguousarray(np.asarray(L0_attn_Wv, dtype=np.float32))
    wo = np.ascontiguousarray(np.asarray(L0_attn_Wo, dtype=np.float32))
    bq = np.asarray(L0_attn_bq, dtype=np.float32)
    bk = np.asarray(L0_attn_bk, dtype=np.float32)
    bv = np.asarray(L0_attn_bv, dtype=np.float32)
    bo = np.asarray(L0_attn_bo, dtype=np.float32)
    wg = np.asarray(L0_ffn_wq_gate, dtype=np.int32)
    wu = np.asarray(L0_ffn_wq_up, dtype=np.int32)
    wd = np.asarray(L0_ffn_wq_down, dtype=np.int32)
    sg = np.asarray(L0_ffn_s_gate, dtype=np.float32)
    su = np.asarray(L0_ffn_s_up, dtype=np.float32)
    sd = np.asarray(L0_ffn_s_down, dtype=np.float32)

    wgH = np.ascontiguousarray(
        wg.astype(np.uint8).reshape(44, 128, 16, 128).transpose(3, 0, 2, 1))
    wuH = np.ascontiguousarray(
        wu.astype(np.uint8).reshape(44, 128, 16, 128).transpose(3, 0, 2, 1))
    wdH = np.ascontiguousarray(
        wd.astype(np.uint8).reshape(8, 256, 11, 4, 128).transpose(4, 0, 2, 3, 1))
    bq_t = np.ascontiguousarray(bq.reshape(16, 128).T)
    bk_t = np.ascontiguousarray(bk.reshape(4, 128).T)
    bv_bc = np.ascontiguousarray(np.broadcast_to(bv, (128, 512)))
    bo_bc = np.ascontiguousarray(np.broadcast_to(bo, (128, 2048)))
    sg_t = np.ascontiguousarray(sg.reshape(44, 128).T)
    su_t = np.ascontiguousarray(su.reshape(44, 128).T)
    sd_bc = np.ascontiguousarray(np.broadcast_to(sd, (128, 2048)))
    kcos, ksin = _rope_tables(np.arange(T))
    ident = np.eye(128, dtype=np.float32)

    in_maps = []
    for cid in range(8):
        b, c = cid // 4, cid % 4
        qcos, qsin = _rope_tables(4 * np.arange(OWN) + c)
        kp = np.arange(128)[:, None]
        jq = np.arange(128)[None, :]
        mask = np.zeros((128, 512), dtype=np.float32)
        for dblk in range(4):
            mask[:, dblk * 128:(dblk + 1) * 128] = np.where(
                128 * dblk + kp <= 4 * jq + c, 0.0, -1e9).astype(np.float32)
        sel = np.zeros((128, 32), dtype=np.float32)
        sel[4 * np.arange(32) + c, np.arange(32)] = 1.0
        in_maps.append(dict(
            x_full=np.ascontiguousarray(x[b]),
            x_own=np.ascontiguousarray(x[b, c::4]),
            wq=wq, wk=wk, wv=wv, wo=wo, bq_t=bq_t, bk_t=bk_t, bv_bc=bv_bc,
            bo_bc=bo_bc, wg8=wgH, wu8=wuH, wd8=wdH, sg_t=sg_t, su_t=su_t,
            sd_bc=sd_bc, kcos=kcos, ksin=ksin, qcos=qcos, qsin=qsin,
            masks=mask.astype(ml_dtypes.bfloat16), ident=ident, sel=sel,
        ))

    return in_maps


def kernel(**inputs):
    in_maps = host_inmaps(**inputs)
    nc = build_program()
    res = run_bass_kernel_spmd(nc, in_maps, core_ids=list(range(8)))
    full = np.empty((2, T, D), dtype=np.float32)
    for cid in range(8):
        b, c = cid // 4, cid % 4
        full[b, c::4] = res.results[cid]["out"]
    return full


if __name__ == "__main__":
    build_program()
    print("built ok")
